# version 20
# baseline (speedup 1.0000x reference)
import contextlib
import numpy as np
import concourse.bass as bass
import concourse.mybir as mybir

F32 = mybir.dt.float32
BF16 = mybir.dt.bfloat16
AF = mybir.ActivationFunctionType
ALU = mybir.AluOpType
AX = mybir.AxisListType

ENGINES = ("tensor", "vector", "scalar", "gpsimd", "sync")


class Prog:
    def __init__(self, nc, stack, dma_streams=()):
        self.nc = nc
        self.stack = stack
        self.ops = {e: [] for e in ENGINES}
        self.sem = {e: stack.enter_context(nc.semaphore(f"sem_{e}")) for e in ENGINES}
        self.cnt = {e: 0 for e in ENGINES}
        self.dma_sem = {}
        self.dma_cnt = {}
        for name in dma_streams:
            self.dma_sem[name] = stack.enter_context(nc.semaphore(f"d_{name}"))
            self.dma_cnt[name] = 0
        self._waited = {e: {} for e in ENGINES}

    def _wait(self, eng, tok):
        if tok is None:
            return
        sem, val = tok
        key = sem.name if hasattr(sem, "name") else id(sem)
        if self._waited[eng].get(key, -1) >= val:
            return
        self._waited[eng][key] = val
        self.ops[eng].append(lambda h, sem=sem, val=val: h.wait_ge(sem, val))

    def emit(self, eng, fn, waits=(), inc=False):
        for t in waits:
            self._wait(eng, t)
        if inc:
            self.cnt[eng] += 1
            v = self.cnt[eng]
            sem = self.sem[eng]

            def run(h, fn=fn, sem=sem):
                fn(h).then_inc(sem, 1)

            self.ops[eng].append(run)
            return (sem, v)
        self.ops[eng].append(lambda h, fn=fn: fn(h))
        return None

    def dma(self, q, stream, out, in_, waits=(), **kw):
        if stream not in self.dma_sem:
            self.dma_sem[stream] = self.stack.enter_context(self.nc.semaphore(f"d_{stream}"))
            self.dma_cnt[stream] = 0
        for t in waits:
            self._wait(q, t)
        self.dma_cnt[stream] += 16
        v = self.dma_cnt[stream]
        sem = self.dma_sem[stream]
        self.ops[q].append(
            lambda h, out=out, in_=in_, sem=sem: h.dma_start(out=out, in_=in_, **kw).then_inc(sem, 16)
        )
        return (sem, v)

    def etok(self, eng):
        return (self.sem[eng], self.cnt[eng])

    def finalize(self):
        with self.nc.Block() as block:
            for name, deco in (("sync", block.sync), ("gpsimd", block.gpsimd),
                               ("scalar", block.scalar), ("vector", block.vector),
                               ("tensor", block.tensor)):
                if self.ops[name]:
                    ops = self.ops[name]

                    def runner(h, ops=ops):
                        for f in ops:
                            f(h)
                    deco(runner)


def build(T=2048, dbg=()):
    NT = T // 128
    SC = NT
    NTAU = T // 512
    HB = T // 16
    UB = 512 // HB
    NCH = 8
    dbg = set(dbg)
    assert NTAU <= 4

    nc = bass.Bass("TRN2", target_bir_lowering=False)

    def dram_in(name, shape):
        return nc.dram_tensor(name, shape, F32, kind="ExternalInput").ap()

    x_d = dram_in("x", [128, NT * 1024])
    wq_d = dram_in("wq", [4, 128, 8 * 256])
    wo_d = dram_in("wo", [4, 128, 8 * 256])
    wkv_d = dram_in("wkv", [128, 8 * 128])
    bq_d = dram_in("bq", [128, 8])
    bo_d = dram_in("bo", [128, 8])
    bkv_d = dram_in("bkv", [128, 1])
    gw_d = dram_in("gw", [128, 64])
    gb_d = dram_in("gb", [8, 1])
    b1_d = dram_in("b1", [128, 128])
    b2_d = dram_in("b2", [8, 1024])
    w1_d = dram_in("w1", [16, 4, 128, 8 * 256])
    w2_d = dram_in("w2", [16, 4, 128, 8 * 256])
    y_d = nc.dram_tensor("y", [128, NT * 1024], F32, kind="ExternalOutput").ap()

    dbg_out = {}
    def dbg_dram(name, shape, dtype=F32):
        dbg_out[name] = nc.dram_tensor("dbg_" + name, shape, dtype, kind="ExternalOutput").ap()
        return dbg_out[name]

    with contextlib.ExitStack() as stack:
        OFF = [16768]
        def sb(name, shape, dtype=F32, at=None):
            nbytes = int(np.prod(shape[1:])) * mybir.dt.size(dtype)
            if at is None:
                at = OFF[0]
            OFF[0] = max(OFF[0], at + ((nbytes + 31) // 32) * 32)
            return nc.alloc_sbuf_tensor_at(name, list(shape), dtype, offset=at)

        x_tm = sb("x_tm", [128, NT * 1024], F32)
        r1 = OFF[0]
        xn1_fm = sb("xn1_fm", [128, NCH, T], BF16)
        A_fm = sb("A_fm", [128, NCH, T], BF16, at=r1)
        Hs = sb("Hs", [128, NCH, T], BF16, at=r1)
        r2 = OFF[0]
        q_fm = sb("q_fm", [128, NCH, T], BF16)
        xn2_fm = sb("xn2_fm", [128, NCH, T], BF16, at=r2)
        r3 = OFF[0]
        PT_A = sb("PT_A", [128, SC * 512], BF16)
        PT_B = sb("PT_B", [128, SC * 512], BF16)
        o_acc = sb("o_acc", [128, NCH, T], BF16, at=r3)
        qsq = sb("qsq", [128, 512], BF16, at=r3)
        qsq2 = sb("qsq2", [128, 512], BF16, at=r3 + 1024)
        sqd = sb("sqd", [128, 1024], BF16, at=r3 + 2048)
        s_big = sb("s_big", [128, 2 * NTAU * 512], BF16, at=r3 + 4096)
        woq23 = sb("woq23", [128, 2 * 2048], BF16, at=r3 + 12288)
        wb1b = [sb("wb1b0", [128, 8 * 256], BF16, at=r3 + 20480),
                sb("wb1b1", [128, 8 * 256], BF16, at=r3 + 24576)]
        r4 = OFF[0]
        wqs = [sb("wqs0", [128, 8 * 256], BF16), sb("wqs1", [128, 8 * 256], BF16)]
        wos = [sb("wos0", [128, 8 * 256], BF16, at=r4),
               sb("wos1", [128, 8 * 256], BF16, at=r4 + 4096)]
        wb1 = [sb("wb1_0", [128, 8 * 256], BF16, at=r4),
               sb("wb1_1", [128, 8 * 256], BF16, at=r4 + 4096)]
        r5 = OFF[0]
        kv_fm = sb("kv_fm", [128, T], BF16)
        X0 = OFF[0]
        khi = sb("khi", [128, T], BF16, at=X0)
        O_st = sb("O_st", [64, 16 * 512], BF16, at=X0)
        wb2 = [sb("wb2_0", [128, 8 * 256], BF16, at=X0),
               sb("wb2_1", [128, 8 * 256], BF16, at=X0 + 4096)]
        O_hi = sb("O_hi", [128, 16 * 256], BF16, at=X0 + 4096)
        w_eb = sb("w_eb", [128, T], BF16, at=X0 + 8192)
        tmpb = sb("tmpb", [128, T], BF16, at=X0 + 8192 + 2 * T)
        OFF[0] = max(OFF[0], X0 + 16384, X0 + 8192 + 4 * T)
        v_aug = sb("v_aug", [128, SC * 65], BF16)
        wkvs = sb("wkvs", [128, 8 * 128], BF16, at=OFF[0] - ((SC * 65 * 2 + 31) // 32) * 32)
        xns_base = OFF[0]
        xns = [sb("xns0", [128, 1024], BF16), sb("xns1", [128, 1024], BF16)]
        W_em = sb("W_em", [8, T], BF16, at=xns_base)
        rs = OFF[0]
        zr_st = sb("zr_st", [65, 512], BF16)
        zinv32 = sb("zinv32", [16, 512], F32)
        e16t = sb("e16t", [16, 16 * 128], BF16)
        attn_end = OFF[0]
        OFF[0] = rs
        Gtm = sb("Gtm", [128, NT * 8], F32)
        probs = sb("probs", [128, NT * 8], F32)
        Wtm = sb("Wtm", [128, NT * 8], BF16)
        b2T = sb("b2T", [8, 1024], BF16)
        e8 = sb("e8", [8, 8 * 128], BF16)
        OFF[0] = max(OFF[0], attn_end)
        gm1 = sb("gm1", [128, NT], F32)
        gm1n = sb("gm1n", [128, NT], F32)
        gm2 = sb("gm2", [128, NT], F32)
        gtmp = sb("gtmp", [128, NT * 8], F32)
        G_sb = sb("G_sb", [8, 256], F32)
        ident = sb("ident", [128, 128], BF16)
        identf = sb("identf", [8, 8], F32)
        onesb = sb("onesb", [128, 128], BF16)
        zsel16 = sb("zsel16", [65, 256], BF16)
        qsel = sb("qsel", [128, 16], BF16)
        stt4 = sb("stt4", [4, 1], F32)
        gws = sb("gws", [128, 64], BF16)
        bq_s = sb("bq_s", [128, 8], F32)
        bo_s = sb("bo_s", [128, 8], F32)
        bkv_s = sb("bkv_s", [128, 1], F32)
        gb_s = sb("gb_s", [8, 1], F32)
        b1_s = sb("b1_s", [128, 128], F32)
        sums = sb("sums", [128, NT], F32)
        sqs = sb("sqs", [128, NT], F32)
        mean = sb("mean", [128, NT], F32)
        inv = sb("inv", [128, NT], F32)

        stt1 = sb("stt1", [1, 8], F32)
        negb = sb("negb", [1, 1], BF16)
        negB_b = sb("negB_b", [128, 1], F32)

        used = OFF[0]
        assert used <= 229376, f"SBUF over budget: {used}"

        pw = stack.enter_context(nc.psum_tensor("pw", [128, 8 * 512], F32))
        pwb = pw.bitcast(BF16)

        def bank(i, lo=0, hi=128, w=512):
            return pw[lo:hi, i * 512:i * 512 + w]

        def bankb(i, lo=0, hi=128, w=1024):
            return pwb[lo:hi, i * 1024:i * 1024 + w]

        streams = ["cst", "sml", "wqa", "wqb", "woa", "wob",
                   "w1a", "w1b", "w2a", "w2b", "kvcp", "out"]
        p = Prog(nc, stack, dma_streams=streams)
        dmp = {}
        qstat_mm = {}
        qstat_bank = {}
        sq_slot_tok = [None, None]
        sq_slot_wr = [None, None]
        bank_tok = [None] * 8

        CH4 = NT * 1024 // 4
        tk_xc = [p.dma("sync", f"x{i}", x_tm[:, i * CH4:(i + 1) * CH4],
                       x_d[:, i * CH4:(i + 1) * CH4]) for i in range(4)]
        for dst, src in ((bq_s, bq_d), (bo_s, bo_d), (bkv_s, bkv_d), (gb_s, gb_d), (b1_s, b1_d)):
            p.dma("sync", "cst", dst[:, :], src)
        tk_cst = (p.dma_sem["cst"], p.dma_cnt["cst"])
        p.dma("gpsimd", "sml", wkvs[:, :], wkv_d)
        p.dma("gpsimd", "sml", gws[:, :], gw_d)
        tk_sml = (p.dma_sem["sml"], p.dma_cnt["sml"])

        tk_ones = p.emit("gpsimd", lambda h: h.memset(onesb[:, :], 1.0), inc=True)
        tk_id = p.emit("gpsimd", lambda h: h.affine_select(
            out=ident[:, :], in_=onesb[:, :], pattern=[[1, 128]],
            compare_op=ALU.is_equal, fill=0.0, base=0, channel_multiplier=-1),
            waits=[tk_ones], inc=True)
        tk_if0 = p.emit("gpsimd", lambda h: h.memset(identf[:, :], 1.0), inc=True)
        tk_idf = p.emit("gpsimd", lambda h: h.affine_select(
            out=identf[:, :], in_=identf[:, :], pattern=[[1, 8]],
            compare_op=ALU.is_equal, fill=0.0, base=0, channel_multiplier=-1),
            waits=[tk_if0], inc=True)
        tk_qs0 = p.emit("gpsimd", lambda h: h.memset(qsel[:, :], 1.0), inc=True)
        tk_qsel = p.emit("gpsimd", lambda h: h.affine_select(
            out=qsel[:, :], in_=qsel[:, :], pattern=[[1, 4], [-1, 4]],
            compare_op=ALU.is_equal, fill=0.0, base=0, channel_multiplier=0),
            waits=[tk_qs0], inc=True)
        tk_z0 = p.emit("gpsimd", lambda h: h.memset(zsel16[:, :], 1.0), inc=True)
        tk_zsel = p.emit("gpsimd", lambda h: h.affine_select(
            out=zsel16[:, :], in_=zsel16[:, :], pattern=[[1, 16], [-1, 16]],
            compare_op=ALU.is_equal, fill=0.0, base=0, channel_multiplier=0),
            waits=[tk_z0], inc=True)
        tk_e0 = p.emit("gpsimd", lambda h: h.memset(e16t[:, :], 1.0), inc=True)
        tk_e16 = p.emit("gpsimd", lambda h: h.affine_select(
            out=e16t[:, :], in_=e16t[:, :], pattern=[[1, 16], [0, 128]],
            compare_op=ALU.is_equal, fill=0.0, base=0, channel_multiplier=-1),
            waits=[tk_e0], inc=True)
        tk_bq = p.emit("vector", lambda h: h.tensor_scalar_mul(bq_s[:, :], bq_s[:, :], 0.125),
                       waits=[tk_cst], inc=True)

        tk_wq = [None] * 4
        tk_wq[0] = p.dma("gpsimd", "wqa", wqs[0][:, :], wq_d[0])
        tk_wq[1] = p.dma("gpsimd", "wqb", wqs[1][:, :], wq_d[1])

        def ln(xn_dst, x_tok, sq_war=None, x_toks=None):
            TPG = max(NT // NTAU, 1)
            xt = (lambda i: x_toks[min(i * 4 // NT, len(x_toks) - 1)] if len(x_toks) == 4
                  else x_toks[i]) if x_toks else (lambda i: x_tok)
            sum_tok = [None] * NTAU
            sq_tok = [None] * NTAU
            inv_tok = [None] * NTAU
            tk_sq = sq_war
            for g in range(NTAU):
                for i in range(g * TPG, (g + 1) * TPG):
                    sum_tok[g] = p.emit("vector", lambda h, i=i: h.reduce_sum(
                        sums[:, i:i + 1], x_tm[:, i * 1024:(i + 1) * 1024], AX.X),
                        waits=[xt(i)], inc=(i == (g + 1) * TPG - 1))
                for i in range(g * TPG, (g + 1) * TPG):
                    tk_sq = p.emit("scalar", lambda h, i=i: h.activation(
                        sqd[:, :], x_tm[:, i * 1024:(i + 1) * 1024], AF.Square,
                        accum_out=sqs[:, i:i + 1]),
                        waits=[xt(i), tk_sq], inc=True)
                sq_tok[g] = tk_sq
                sl = slice(g * TPG, (g + 1) * TPG)
                tk_mean = p.emit("vector", lambda h, sl=sl: h.tensor_scalar_mul(
                    mean[:, sl], sums[:, sl], 1.0 / 1024),
                    waits=[sum_tok[g]], inc=True)
                tk_m2 = p.emit("vector", lambda h, sl=sl: h.tensor_mul(
                    inv[:, sl], mean[:, sl], mean[:, sl]), waits=[tk_mean], inc=True)
                tk_var = p.emit("vector", lambda h, sl=sl: h.scalar_tensor_tensor(
                    inv[:, sl], sqs[:, sl], 1.0 / 1024, inv[:, sl], ALU.mult, ALU.subtract),
                    waits=[tk_m2, sq_tok[g]], inc=True)
                tk_eps = p.emit("vector", lambda h, sl=sl: h.tensor_scalar_add(
                    inv[:, sl], inv[:, sl], 1e-5), waits=[tk_var], inc=True)
                tk_std = p.emit("scalar", lambda h, sl=sl: h.activation(
                    inv[:, sl], inv[:, sl], AF.Sqrt), waits=[tk_eps], inc=True)
                inv_tok[g] = p.emit("vector", lambda h, sl=sl: h.reciprocal(
                    inv[:, sl], inv[:, sl]), waits=[tk_std], inc=True)
            touts = [None] * NT
            for i in range(NT):
                g = i // TPG
                slot = xns[i % 2]
                b = i % 2
                tk_xn = p.emit("vector", lambda h, i=i, slot=slot: h.tensor_scalar(
                    slot[:, :], x_tm[:, i * 1024:(i + 1) * 1024],
                    mean[:, i:i + 1], inv[:, i:i + 1], ALU.subtract, ALU.mult),
                    waits=[inv_tok[g], bank_tok[i % 2], xt(i)], inc=True)
                tts = None
                for c in range(NCH):
                    tts = p.emit("tensor", lambda h, c=c, b=b, slot=slot: h.transpose(
                        bankb(b)[:, c * 128:(c + 1) * 128],
                        slot[:, c * 128:(c + 1) * 128], ident[:, :]),
                        waits=[tk_xn, tk_id, bank_tok[b]], inc=(c == NCH - 1))
                touts[i] = p.emit("vector", lambda h, i=i, b=b: h.tensor_copy(
                    xn_dst[:, :, i * 128:(i + 1) * 128],
                    bankb(b)[:, :].rearrange("p (c k) -> p c k", k=128)),
                    waits=[tts], inc=True)
                bank_tok[b] = touts[i]
            return touts

        tk_xn1 = ln(xn1_fm, tk_xc[-1], x_toks=tk_xc)[-1]
        if "xn1" in dbg:
            dmp["r1"] = p.dma("gpsimd", "dbg_xn1", dbg_dram("xn1", [128, NCH * T], BF16),
                              xn1_fm[:, :, :], waits=[tk_xn1])

        tk_q_evac = None
        wq_slot_last = [None, None]
        qstat_bank_all = bank(5, lo=0, hi=4)
        for c_o in range(NCH):
            qq = c_o // 2
            mmt = [None] * NTAU
            for c_e in range(NCH):
                for tau in range(NTAU):
                    b = tau % 4
                    mmt[tau] = p.emit("tensor", lambda h, c_o=c_o, tau=tau, c_e=c_e, b=b, qq=qq: h.matmul(
                        bank(b), wqs[qq % 2][:, c_e * 256 + (c_o % 2) * 128: c_e * 256 + (c_o % 2) * 128 + 128],
                        xn1_fm[:, c_e, tau * 512:(tau + 1) * 512],
                        start=(c_e == 0), stop=(c_e == NCH - 1)),
                        waits=[tk_xn1, tk_wq[qq], bank_tok[b] if c_e == 0 else None],
                        inc=(c_e == NCH - 1))
            for tau in range(NTAU):
                b = tau % 4
                tk_q_evac = p.emit("scalar", lambda h, c_o=c_o, tau=tau, b=b: h.activation(
                    q_fm[:, c_o, tau * 512:(tau + 1) * 512], bank(b), AF.Identity,
                    bias=bq_s[:, c_o:c_o + 1], scale=0.125),
                    waits=[mmt[tau], tk_bq], inc=True)
                bank_tok[b] = tk_q_evac
                wq_slot_last[qq % 2] = tk_q_evac
                slot = qsq if (c_o % 2 == 0) else qsq2
                tk_sqc = p.emit("scalar", lambda h, c_o=c_o, tau=tau, slot=slot: h.activation(
                    slot[:, :], q_fm[:, c_o, tau * 512:(tau + 1) * 512], AF.Square),
                    waits=[tk_q_evac, sq_slot_tok[c_o % 2], sq_slot_wr[c_o % 2]], inc=True)
                sq_slot_wr[c_o % 2] = tk_sqc
                qstat_mm[tau] = p.emit("tensor", lambda h, c_o=c_o, tau=tau, slot=slot: h.matmul(
                    qstat_bank_all, qsel[:, (tau % 4) * 4:(tau % 4) * 4 + 4], slot[:, :],
                    start=(c_o == 0 and tau == 0), stop=(c_o == NCH - 1 and tau == NTAU - 1)),
                    waits=[tk_sqc, tk_qsel] + ([bank_tok[5]] if c_o == 0 and tau == 0 else []),
                    inc=True)
                sq_slot_tok[c_o % 2] = qstat_mm[tau]
            if c_o % 2 == 1:
                nq = qq + 2
                if nq < 4:
                    tk_wq[nq] = p.dma("gpsimd", "wqa" if nq % 2 == 0 else "wqb",
                                      wqs[nq % 2][:, :], wq_d[nq], waits=[wq_slot_last[nq % 2]])

        tk_wo = [None] * 4
        tk_wo[0] = p.dma("gpsimd", "woa", wos[0][:, :], wo_d[0], waits=[p.etok("tensor")])
        tk_wo[1] = p.dma("gpsimd", "wob", wos[1][:, :], wo_d[1])

        tk_kv_evac = None
        mmt = [None] * NTAU
        for c_e in range(NCH):
            for tau in range(NTAU):
                b = tau % 4
                mmt[tau] = p.emit("tensor", lambda h, tau=tau, c_e=c_e, b=b: h.matmul(
                    bank(b), wkvs[:, c_e * 128:(c_e + 1) * 128],
                    xn1_fm[:, c_e, tau * 512:(tau + 1) * 512],
                    start=(c_e == 0), stop=(c_e == NCH - 1)),
                    waits=[tk_xn1, tk_sml, bank_tok[b] if c_e == 0 else None],
                    inc=(c_e == NCH - 1))
        for tau in range(NTAU):
            b = tau % 4
            tk_kv_evac = p.emit("scalar", lambda h, tau=tau, b=b: h.activation(
                kv_fm[:, tau * 512:(tau + 1) * 512], bank(b), AF.Identity,
                bias=bkv_s[:, 0:1], scale=1.0),
                waits=[mmt[tau], tk_cst], inc=True)
            bank_tok[b] = tk_kv_evac

        tk_vones = p.emit("gpsimd", lambda h: h.memset(
            v_aug[:, :].rearrange("p (j q) -> p j q", q=65)[:, :, 64:65], 1.0),
            waits=[p.etok("tensor")], inc=True)
        tk_kvx2 = p.dma("sync", "kvcp", khi[64:128, :], kv_fm[0:64, :], waits=[tk_kv_evac])
        tk_kvx = (p.dma_sem["kvcp"], p.dma_cnt["kvcp"])

        vts = None
        for j in range(SC):
            vts = p.emit("tensor", lambda h, j=j: h.transpose(
                bankb(4)[:, j * 64:(j + 1) * 64],
                kv_fm[64:128, j * 128:(j + 1) * 128], ident[64:128, 64:128]),
                waits=[tk_kv_evac, tk_id, bank_tok[4]], inc=(j == SC - 1))
        tk_vaug = p.emit("vector", lambda h: h.tensor_copy(
            v_aug[:, :].rearrange("p (j q) -> p j q", q=65)[:, :, 0:64],
            bankb(4, w=SC * 64)[:, :].rearrange("p (j q) -> p j q", q=64)),
            waits=[vts, tk_vones], inc=True)
        bank_tok[4] = tk_vaug

        tk_qmax = tk_kmax = None
        for which in range(2):
            src_is_q = (which == 0)
            if src_is_q:
                tk_r = p.emit("vector", lambda h: h.reduce_max(
                    stt4[0:4, 0:1], bank(5, lo=0, hi=4), AX.X),
                    waits=[qstat_mm[NTAU - 1]], inc=True)
                bank_tok[5] = tk_r
                tk_tr = p.emit("tensor", lambda h: h.transpose(
                    bank(6, lo=0, hi=1, w=4), stt4[0:4, 0:1], identf[0:4, 0:4]),
                    waits=[tk_r, tk_idf, bank_tok[6]], inc=True)
                tk_qmax = p.emit("vector", lambda h: h.reduce_max(
                    stt1[0:1, 0:1], bank(6, lo=0, hi=1, w=4), AX.X),
                    waits=[tk_tr], inc=True)
                bank_tok[6] = tk_qmax
                continue
            nchunk = 1
            first = True
            tk_r = None
            for tau in range(NTAU):
                mms = None
                for c in range(nchunk):
                    slot = qsq if (c % 2 == 0) else qsq2
                    tk_sqc = p.emit("scalar", lambda h, c=c, tau=tau, slot=slot: h.activation(
                        slot[:, :], kv_fm[:, tau * 512:(tau + 1) * 512], AF.Square),
                        waits=[tk_kv_evac, sq_slot_tok[c % 2], sq_slot_wr[c % 2]], inc=True)
                    sq_slot_wr[c % 2] = tk_sqc
                    mms = p.emit("tensor", lambda h, c=c, nchunk=nchunk, slot=slot: h.matmul(
                        bank(5, lo=0, hi=1), onesb[:, 0:1], slot[:, :],
                        start=(c == 0), stop=(c == nchunk - 1)),
                        waits=[tk_sqc, tk_ones, bank_tok[5]], inc=True)
                    sq_slot_tok[c % 2] = mms
                dst = which if first else which + 2
                tk_r = p.emit("vector", lambda h, dst=dst: h.reduce_max(
                    stt1[0:1, dst:dst + 1], bank(5, lo=0, hi=1), AX.X),
                    waits=[mms, tk_r], inc=True)
                bank_tok[5] = tk_r
                if not first:
                    tk_r = p.emit("vector", lambda h, which=which: h.tensor_max(
                        stt1[0:1, which:which + 1], stt1[0:1, which:which + 1],
                        stt1[0:1, which + 2:which + 3]), waits=[tk_r], inc=True)
                first = False
            tk_kmax = tk_r
        tk_b2v = p.emit("vector", lambda h: h.tensor_mul(stt1[0:1, 4:5], stt1[0:1, 0:1], stt1[0:1, 1:2]),
                        waits=[tk_qmax, tk_kmax], inc=True)
        tk_bsq = p.emit("scalar", lambda h: h.activation(stt1[0:1, 5:6], stt1[0:1, 4:5], AF.Sqrt),
                        waits=[tk_b2v], inc=True)
        tk_nb = p.emit("vector", lambda h: h.tensor_copy(negb[0:1, 0:1], stt1[0:1, 5:6]),
                       waits=[tk_bsq], inc=True)
        tk_nbmm = p.emit("tensor", lambda h: h.matmul(bank(5, lo=0, hi=128, w=1), onesb[0:1, :],
                                                      negb[0:1, 0:1], start=True, stop=True),
                         waits=[tk_nb, bank_tok[5]], inc=True)
        tk_negB = p.emit("scalar", lambda h: h.activation(negB_b[:, :], bank(5, w=1), AF.Copy,
                                                          bias=0.0, scale=-1.0),
                         waits=[tk_nbmm], inc=True)
        bank_tok[5] = tk_negB

        if "qkv" in dbg:
            dmp["q"] = p.dma("gpsimd", "dbg_q", dbg_dram("q", [128, NCH * T], BF16), q_fm[:, :, :], waits=[tk_q_evac])
            dmp["kv"] = p.dma("gpsimd", "dbg_kv", dbg_dram("kv", [128, T], BF16), kv_fm[:, :], waits=[tk_kv_evac])
            dmp["vaug"] = p.dma("gpsimd", "dbg_vaug", dbg_dram("vaug", [128, SC * 65], BF16), v_aug[:, :], waits=[tk_vaug])
            p.dma("gpsimd", "dbg_negB", dbg_dram("negB", [128, 1]), negB_b[:, :], waits=[tk_negB])

        ost_tok = [None] * 16
        ohi_toks = [None] * 16
        zr_hi_tok = {32: None, 64: None}
        zr_lo_tok = None
        zq_bank_tok = None
        sc2_tok = None
        pv_bank_tok = {4: bank_tok[4], 5: bank_tok[5]}
        zq_bank_tok = bank_tok[6]
        dbg_pt_done = [False]

        pv_prev = [None, None]
        scat_queue = []

        def make_scatter(tau, ch, oev_tok, ohi_tok, zib_tok, zbk=7):
            def emit_scatter():
                nonlocal zr_lo_tok
                tk_zb = p.emit("tensor", lambda h, ch=ch, zbk=zbk: h.matmul(
                    bank(zbk), e16t[:, ch * 128:(ch + 1) * 128], zr_st[0:16, 0:512],
                    start=True, stop=True),
                    waits=[zib_tok, tk_e16, bank_tok[zbk]], inc=True)
                zr_lo_tok = tk_zb
                last_rd = None
                UH = UB // 2
                cpr0 = (tau * UB) >> 1
                dstE = A_fm[0:64, :, :].rearrange("p c (k s) -> p c s k", s=16)[
                    :, cpr0:cpr0 + UH, ch, :]
                dstO = A_fm[64:128, :, :].rearrange("p c (k s) -> p c s k", s=16)[
                    :, cpr0:cpr0 + UH, ch, :]
                oev_e = O_st[0:64, ch * 512:(ch + 1) * 512].rearrange(
                    "p (uu k) -> p uu k", k=HB)[:, 0::2, :]
                ohi_o = O_hi[64:128, ch * 256:(ch + 1) * 256].rearrange(
                    "p (uu k) -> p uu k", k=HB)
                zbE = bank(zbk, lo=0, hi=64)[:, :].rearrange(
                    "p (uu k) -> p uu k", k=HB)[:, 0::2, :]
                zbO = bank(zbk, lo=64, hi=128)[:, :].rearrange(
                    "p (uu k) -> p uu k", k=HB)[:, 1::2, :]
                p.emit("vector", lambda h: h.tensor_tensor(dstE, oev_e, zbE, ALU.mult),
                       waits=[tk_zb, oev_tok, dmp.get("r1")], inc=False)
                last_rd = p.emit("vector", lambda h: h.tensor_tensor(dstO, ohi_o, zbO, ALU.mult),
                                 waits=[tk_zb, ohi_tok, dmp.get("r1")], inc=True)
                ost_tok[ch] = last_rd
                bank_tok[zbk] = last_rd
            return emit_scatter

        zbs_war = [None]
        for tau in range(NTAU):
            oev_toks = {}
            zq_last = None
            for c in range(8):
                if scat_queue:
                    scat_queue.pop(0)()
                expA = [None] * SC
                expB = [None] * SC
                pvA = pvB = None
                for j0 in range(0, SC + 2, 2):
                    if j0 == (SC // 2) and scat_queue:
                        scat_queue.pop(0)()
                    for j in (j0, j0 + 1):
                        if j < SC:
                            bA = j % 2
                            bB = 2 + j % 2
                            tkp = p.emit("tensor", lambda h, j=j, tau=tau, c=c, bA=bA: h.matmul(
                                bank(bA), kv_fm[0:64, j * 128:(j + 1) * 128],
                                q_fm[0:64, c, tau * 512:(tau + 1) * 512],
                                start=True, stop=True, tile_position=(0, 0)),
                                waits=[tk_kv_evac, tk_q_evac, bank_tok[bA]], inc=True)
                            tkA2 = tkp
                            if j % 2 == 1:
                                qkA = tkp
                            tkp = p.emit("tensor", lambda h, j=j, tau=tau, c=c, bB=bB: h.matmul(
                                bank(bB), khi[64:128, j * 128:(j + 1) * 128],
                                q_fm[64:128, c, tau * 512:(tau + 1) * 512],
                                start=True, stop=True, tile_position=(64, 0)),
                                waits=[tk_kvx, bank_tok[bB]], inc=True)
                            if j % 2 == 1:
                                qkB = tkp
                            eA = p.emit("scalar", lambda h, j=j, bA=bA: h.activation(
                                PT_A[:, j * 512:(j + 1) * 512], bank(bA), AF.Exp,
                                bias=negB_b[:, 0:1], scale=1.0),
                                waits=[tkA2, tk_negB, pv_prev[0], dmp.get("pt")], inc=True)
                            expA[j] = eA
                            eB = p.emit("scalar", lambda h, j=j, bB=bB: h.activation(
                                PT_B[:, j * 512:(j + 1) * 512], bank(bB), AF.Exp,
                                bias=negB_b[:, 0:1], scale=1.0),
                                waits=[tkp, pv_prev[1], dmp.get("pt")], inc=True)
                            expB[j] = eB
                            bank_tok[bA] = eA
                            bank_tok[bB] = eB
                    for j in (j0 - 2, j0 - 1):
                        if 0 <= j < SC:
                            jj = j
                            pvA = p.emit("tensor", lambda h, jj=jj: h.matmul(
                                bank(4, lo=0, hi=65), v_aug[:, jj * 65:(jj + 1) * 65],
                                PT_A[:, jj * 512:(jj + 1) * 512],
                                start=(jj == 0), stop=(jj == SC - 1)),
                                waits=[tk_vaug, expA[jj], pv_bank_tok[4] if jj == 0 else None],
                                inc=(jj == SC - 1))
                    for j in (j0 - 2, j0 - 1):
                        if 0 <= j < SC:
                            jj = j
                            pvB = p.emit("tensor", lambda h, jj=jj: h.matmul(
                                bank(5, lo=0, hi=65), v_aug[:, jj * 65:(jj + 1) * 65],
                                PT_B[:, jj * 512:(jj + 1) * 512],
                                start=(jj == 0), stop=(jj == SC - 1)),
                                waits=[expB[jj], pv_bank_tok[5] if jj == 0 else None],
                                inc=(jj == SC - 1))
                if "pt" in dbg and tau == 0 and c == 0 and not dbg_pt_done[0]:
                    dbg_pt_done[0] = True
                    dmp["pt"] = p.dma("gpsimd", "dbg_ptA", dbg_dram("ptA", [128, SC * 512], BF16),
                                      PT_A[:, :], waits=[pvA])
                pv_prev = [pvA, pvB]
                for ch, tk_pv, bpv in ((2 * c, pvA, 4), (2 * c + 1, pvB, 5)):
                    zrow = 64 if ch % 2 == 0 else 32
                    tk_oev = p.emit("vector", lambda h, ch=ch, bpv=bpv: h.tensor_copy(
                        O_st[0:64, ch * 512:ch * 512 + 512], bank(bpv, lo=0, hi=64)),
                        waits=[tk_pv, ost_tok[ch]], inc=True)
                    oev_toks[ch] = tk_oev
                    if UB >= 2:
                        ohi_toks[ch] = p.dma("sync", f"oh{ch}",
                            O_hi[64:128, ch * 256:(ch + 1) * 256].rearrange(
                                "p (uu k) -> p uu k", k=HB),
                            O_st[0:64, ch * 512:(ch + 1) * 512].rearrange(
                                "p (uu k) -> p uu k", k=HB)[:, 1::2, :],
                            waits=[tk_oev, ost_tok[ch]])
                    tk_zev = p.emit("vector", lambda h, bpv=bpv, zrow=zrow: h.tensor_copy(
                        zr_st[zrow:zrow + 1, :], bank(bpv, lo=64, hi=65)),
                        waits=[tk_pv, zr_hi_tok[zrow]], inc=True)
                    tk_zq = p.emit("tensor", lambda h, ch=ch, zrow=zrow: h.matmul(
                        bank(6, lo=0, hi=16), zsel16[zrow:zrow + 1, ch * 16:(ch + 1) * 16],
                        zr_st[zrow:zrow + 1, :],
                        start=(ch == 0), stop=(ch == 15)),
                        waits=[tk_zev, tk_zsel] + ([zq_bank_tok] if ch == 0 else []),
                        inc=True)
                    zr_hi_tok[zrow] = tk_zq
                    zq_last = tk_zq
                    pv_bank_tok[bpv] = tk_zev
            tk_zi = p.emit("vector", lambda h: h.reciprocal(zinv32[:, :], bank(6, lo=0, hi=16)),
                           waits=[zq_last, dmp.get("zq")], inc=True)
            zq_bank_tok = tk_zi
            tk_zib = p.emit("vector", lambda h: h.tensor_copy(zr_st[0:16, 0:512], zinv32[:, :]),
                            waits=[tk_zi, zr_lo_tok], inc=True)
            if "zq" in dbg and tau == 0:
                dmp["zq"] = p.dma("gpsimd", "dbg_zinv", dbg_dram("zinv", [16, 512]), zinv32[:, :], waits=[tk_zi])
            for ch in range(16):
                zbk = 7 if tau < NTAU - 1 else ch % 8
                scat_queue.append(make_scatter(tau, ch, oev_toks[ch], ohi_toks[ch], tk_zib, zbk))
        afm_partial = p.etok("vector")
        tk_pt_dead = p.etok("tensor")
        last_scat_tok = {}
        ich = 0
        while scat_queue:
            scat_queue.pop(0)()
            last_scat_tok[ich] = ost_tok[ich]
            ich += 1
        tk_afm_done = p.etok("vector")
        wb20_free = last_scat_tok.get(7, tk_afm_done)
        tk_b2T = p.dma("gpsimd", "sml2", b2T[:, :], b2_d,
                       waits=[tk_afm_done, p.etok("tensor")])
        tk_e80 = p.emit("gpsimd", lambda h: h.memset(e8[:, :], 1.0),
                        waits=[tk_afm_done, p.etok("tensor")], inc=True)
        tk_e8 = p.emit("gpsimd", lambda h: h.affine_select(
            out=e8[:, :], in_=e8[:, :], pattern=[[1, 8], [0, 128]],
            compare_op=ALU.is_equal, fill=0.0, base=0, channel_multiplier=-1),
            waits=[tk_e80], inc=True)
        if "afm" in dbg:
            dmp["afm"] = p.dma("gpsimd", "dbg_afm", dbg_dram("afm", [128, NCH * T], BF16), A_fm[:, :, :],
                               waits=[tk_afm_done])

        wo_slot_last = [None, None]
        res_tok = [None] * NTAU
        wo_epi = []

        def make_wo_epi2(c_o, tau, ev_tok):
            def emit_epi():
                bt = 4 + (c_o + tau) % 2
                tts = None
                for ss in range(4):
                    tts = p.emit("tensor", lambda h, c_o=c_o, tau=tau, ss=ss, bt=bt: h.transpose(
                        bankb(bt)[:, ss * 128:(ss + 1) * 128],
                        s_big[:, (c_o % 2) * NTAU * 512 + tau * 512 + ss * 128:
                              (c_o % 2) * NTAU * 512 + tau * 512 + (ss + 1) * 128],
                        ident[:, :]),
                        waits=[ev_tok, bank_tok[bt]], inc=(ss == 3))
                xv = x_tm[:, tau * 4096:(tau + 1) * 4096].rearrange(
                    "p (s r) -> p s r", r=1024)[:, :, c_o * 128:(c_o + 1) * 128]
                tk_r = p.emit("vector", lambda h, xv=xv, bt=bt: h.tensor_add(
                    xv, xv, bankb(bt)[:, 0:512].rearrange("p (s k) -> p s k", k=128)),
                    waits=[tts], inc=True)
                bank_tok[bt] = tk_r
                res_tok[tau] = tk_r
                sbig_war[c_o % 2] = p.etok("tensor")
            return emit_epi

        sbig_war = [None, None]
        tk_wo23 = p.dma("gpsimd", "wo2", woq23[:, 0:2048], wo_d[2],
                        waits=[p.etok("tensor"), p.etok("vector")])
        tk_wo23b = p.dma("gpsimd", "wo3", woq23[:, 2048:4096], wo_d[3],
                        waits=[p.etok("tensor"), p.etok("vector")])
        for tau0 in range(0, NTAU, 2):
            TP = min(2, NTAU - tau0)
            for c_o in range(NCH):
                qq = c_o // 2
                if qq < 2:
                    lsrc = wos[qq % 2]
                    lbase = 0
                    wtok = tk_wo[qq]
                else:
                    lsrc = woq23
                    lbase = (qq - 2) * 2048
                    wtok = tk_wo23 if qq == 2 else tk_wo23b
                mmt2 = [None] * TP
                for c_a in range(NCH):
                    for dt in range(TP):
                        tau = tau0 + dt
                        b = (c_o % 2) * 2 + dt
                        mmt2[dt] = p.emit("tensor", lambda h, c_o=c_o, tau=tau, c_a=c_a, b=b,
                                          lsrc=lsrc, lbase=lbase: h.matmul(
                            bank(b), lsrc[:, lbase + c_a * 256 + (c_o % 2) * 128:
                                          lbase + c_a * 256 + (c_o % 2) * 128 + 128],
                            A_fm[:, c_a, tau * 512:(tau + 1) * 512],
                            start=(c_a == 0), stop=(c_a == NCH - 1)),
                            waits=[afm_partial if c_a < 2 * (NTAU - 1) else tk_afm_done,
                                   wtok, bank_tok[b] if c_a == 0 else None],
                            inc=(c_a == NCH - 1))
                for dt in range(TP):
                    tau = tau0 + dt
                    b = (c_o % 2) * 2 + dt
                    tk_ev = p.emit("scalar", lambda h, c_o=c_o, tau=tau, b=b: h.activation(
                        s_big[:, (c_o % 2) * NTAU * 512 + tau * 512:
                              (c_o % 2) * NTAU * 512 + (tau + 1) * 512],
                        bank(b), AF.Identity, bias=bo_s[:, c_o:c_o + 1], scale=1.0),
                        waits=[mmt2[dt], tk_cst, sbig_war[c_o % 2]], inc=True)
                    bank_tok[b] = tk_ev
                    wo_epi.append(make_wo_epi2(c_o, tau, tk_ev))
                while len(wo_epi) > 1:
                    wo_epi.pop(0)()
        while wo_epi:
            wo_epi.pop(0)()
        tk_x1 = p.etok("vector")
        if "attn" in dbg:
            dmp["x1"] = p.dma("gpsimd", "dbg_x1", dbg_dram("x1", [128, NT * 1024]), x_tm[:, :], waits=[tk_x1])

        if dmp.get("q") is not None:
            p.emit("vector", lambda h: h.wait_ge(dmp["q"][0], dmp["q"][1]))
        xn2_touts = ln(xn2_fm, tk_x1, sq_war=p.etok("scalar"),
                       x_toks=[res_tok[i // max(NT // NTAU, 1)] for i in range(NT)])
        TPG2 = max(NT // NTAU, 1)
        xn2_tok = [xn2_touts[(g + 1) * TPG2 - 1] for g in range(NTAU)]
        tk_xn2 = xn2_touts[-1]

        tk_gt = None
        gslot_war = [None, None]
        gate_mm_toks = {}
        gate_cl = []

        def g_mm(tau):
            def f():
                b = tau % 2
                mms = None
                for c_e in range(NCH):
                    mms = p.emit("tensor", lambda h, tau=tau, c_e=c_e, b=b: h.matmul(
                        bank(b, lo=0, hi=8), gws[:, c_e * 8:(c_e + 1) * 8],
                        xn2_fm[:, c_e, tau * 512:(tau + 1) * 512],
                        start=(c_e == 0), stop=(c_e == NCH - 1)),
                        waits=[xn2_tok[tau], tk_sml, bank_tok[b]], inc=(c_e == NCH - 1))
                gate_mm_toks[tau] = mms
                bank_tok[b] = mms
            return f

        def g_tr(tau):
            def f():
                nonlocal tk_gt
                b = tau % 2
                tk_ge = None
                for s4 in range(4):
                    i = tau * 4 + s4
                    sl = i % 2
                    b2 = 2 + i % 2
                    tk_ge = p.emit("scalar", lambda h, s4=s4, sl=sl, b=b: h.activation(
                        G_sb[0:8, sl * 128:(sl + 1) * 128],
                        bank(b, lo=0, hi=8)[:, s4 * 128:(s4 + 1) * 128], AF.Identity,
                        bias=gb_s[0:8, 0:1], scale=1.0),
                        waits=[gate_mm_toks[tau], tk_cst, gslot_war[sl]], inc=True)
                    tts = p.emit("tensor", lambda h, sl=sl, b2=b2: h.transpose(
                        bank(b2, lo=0, hi=128, w=8), G_sb[0:8, sl * 128:(sl + 1) * 128],
                        identf[0:8, 0:8]),
                        waits=[tk_ge, tk_idf, bank_tok[b2]], inc=True)
                    gslot_war[sl] = tts
                    tk_gt = p.emit("vector", lambda h, i=i, b2=b2: h.tensor_copy(
                        Gtm[:, i * 8:(i + 1) * 8], bank(b2, w=8)),
                        waits=[tts, tk_afm_done], inc=True)
                    bank_tok[b2] = tk_gt
                bank_tok[b] = tk_ge
            return f

        def g_soft():
            nonlocal tk_gt
            GV = Gtm[:, :].rearrange("p (b j) -> p b j", j=8)
            PV_ = probs[:, :].rearrange("p (b j) -> p b j", j=8)
            WV = gtmp[:, :].rearrange("p (b j) -> p b j", j=8)
            def bc(t):
                return t[:, :].unsqueeze(2).broadcast_to([128, NT, 8])
            t1 = p.emit("vector", lambda h: h.reduce_max(gm1[:, :], GV, AX.X),
                        waits=[tk_gt], inc=True)
            t2 = p.emit("vector", lambda h: h.tensor_tensor(PV_, GV, bc(gm1), ALU.subtract),
                        waits=[t1], inc=True)
            t3 = p.emit("scalar", lambda h: h.activation(probs[:, :], probs[:, :], AF.Exp),
                        waits=[t2], inc=True)
            t4 = p.emit("vector", lambda h: h.reduce_sum(gm2[:, :], PV_, AX.X),
                        waits=[t3], inc=True)
            t5 = p.emit("vector", lambda h: h.reciprocal(gm1[:, :], gm2[:, :]),
                        waits=[t4], inc=True)
            t6 = p.emit("vector", lambda h: h.tensor_tensor(PV_, PV_, bc(gm1), ALU.mult),
                        waits=[t5], inc=True)
            t7 = p.emit("vector", lambda h: h.reduce_max(gm1[:, :], PV_, AX.X),
                        waits=[t6], inc=True)
            t8 = p.emit("vector", lambda h: h.tensor_tensor(WV, PV_, bc(gm1), ALU.is_ge),
                        waits=[t7], inc=True)
            t9 = p.emit("vector", lambda h: h.scalar_tensor_tensor(
                gtmp[:, :], gtmp[:, :], -1e30, probs[:, :], ALU.mult, ALU.add),
                waits=[t8], inc=True)
            t10 = p.emit("vector", lambda h: h.reduce_max(gm2[:, :], WV, AX.X),
                         waits=[t9], inc=True)
            t11 = p.emit("vector", lambda h: h.tensor_tensor(WV, PV_, bc(gm2), ALU.is_ge),
                         waits=[t10], inc=True)
            t12 = p.emit("vector", lambda h: h.tensor_tensor(
                Wtm[:, :], probs[:, :], gtmp[:, :], ALU.mult),
                waits=[t11], inc=True)
            return t12

        def g_wtmT(i0, tk_wtm):
            def f():
                nonlocal tk_wem
                for i in range(i0, min(i0 + 4, NT)):
                    b = i % 2
                    tts = p.emit("tensor", lambda h, i=i, b=b: h.transpose(
                        bankb(b)[0:8, 0:128], Wtm[:, i * 8:(i + 1) * 8], ident[:, :]),
                        waits=[tk_wtm, bank_tok[b]], inc=True)
                    tk_wem = p.emit("vector", lambda h, i=i, b=b: h.tensor_copy(
                        W_em[0:8, i * 128:(i + 1) * 128], bankb(b)[0:8, 0:128]),
                        waits=[tts, tk_afm_done], inc=True)
                    bank_tok[b] = tk_wem
            return f

        tk_wem = None

        def g_full(tau):
            def f():
                g_mm(tau)()
                g_tr(tau)()
            return f

        for tau in range(NTAU):
            gate_cl.append(g_full(tau))

        tk_seed = None
        def emit_web(e):
            tk_webl = None
            for tau in range(NTAU):
                b = tau % 2
                tk_wm = p.emit("tensor", lambda h, e=e, tau=tau, b=b: h.matmul(
                    bank(b), e8[0:8, e * 128:(e + 1) * 128],
                    W_em[0:8, tau * 512:(tau + 1) * 512],
                    start=True, stop=True),
                    waits=[tk_wem, tk_e8, bank_tok[b]], inc=True)
                tk_webl = p.emit("vector", lambda h, tau=tau, b=b: h.tensor_copy(
                    w_eb[:, tau * 512:(tau + 1) * 512], bank(b)),
                    waits=[tk_wm, tk_web_war, tk_afm_done], inc=True)
                bank_tok[b] = tk_webl
            return tk_webl

        tk_w1 = {}
        tk_w2 = {}
        w1_slot_last = [None, None]
        w2_slot_last = [None, None]

        def w1_stream(he, q, *wait_toks):
            buf = wb1b[q % 2] if he == 0 else wb1[q % 2]
            st = ("w1c" if q % 2 == 0 else "w1d") if he == 0 else ("w1a" if q % 2 == 0 else "w1b")
            tk_w1[(he, q)] = p.dma("gpsimd", st, buf[:, :], w1_d[he, q], waits=list(wait_toks))

        def w2_stream(he, q, *wait_toks):
            tk_w2[(he, q)] = p.dma("gpsimd", "w2a" if q % 2 == 0 else "w2b",
                                   wb2[q % 2][:, :], w2_d[he, q], waits=list(wait_toks))

        tk_free_r4 = p.etok("tensor")
        w1_stream(0, 0, tk_pt_dead, dmp.get("kv"), dmp.get("vaug"))
        w1_stream(0, 1, tk_pt_dead, dmp.get("kv"), dmp.get("vaug"))
        w2_stream(0, 0, wb20_free, p.etok("tensor"), dmp.get("kv"), dmp.get("vaug"))
        w2_stream(0, 1, tk_free_r4, tk_afm_done, dmp.get("kv"), dmp.get("vaug"))

        tk_hs_last = None
        tk_oacc = None
        tk_web_war = None
        oacc_co_tok = {}
        seed_toks = {}

        def emit_final_res(c_o, grp_free):
            for tau in range(NTAU):
                bt = grp_free + tau % 4
                tts = None
                for ss in range(4):
                    tts = p.emit("tensor", lambda h, c_o=c_o, tau=tau, ss=ss, bt=bt: h.transpose(
                        bankb(bt)[:, ss * 128:(ss + 1) * 128],
                        o_acc[:, c_o, tau * 512 + ss * 128: tau * 512 + (ss + 1) * 128],
                        ident[:, :]),
                        waits=[oacc_co_tok[c_o], bank_tok[bt]], inc=(ss == 3))
                xv = x_tm[:, tau * 4096:(tau + 1) * 4096].rearrange(
                    "p (s r) -> p s r", r=1024)[:, :, c_o * 128:(c_o + 1) * 128]
                tk_r2 = p.emit("vector", lambda h, xv=xv, bt=bt: h.tensor_add(
                    xv, xv, bankb(bt)[:, 0:512].rearrange("p (s k) -> p s k", k=128)),
                    waits=[tts], inc=True)
                bank_tok[bt] = tk_r2

        for he in range(16):
            e = he // 2
            half = he % 2
            if half == 0 and he > 0:
                tk_web = emit_web(e)
            hs_tok = {}
            for c_h in range(8):
                if he == 0:
                    if gate_cl:
                        gate_cl.pop(0)()
                    if not gate_cl and c_h == 4:
                        tk_wtm_f = g_soft()
                        for i0 in range(0, NT, 4):
                            gate_cl.append(g_wtmT(i0, tk_wtm_f))
                q = c_h // 2
                grp = (c_h % 2) * 4
                mmt = [None] * NTAU
                for c_f in range(NCH):
                    for tau in range(NTAU):
                        b = grp + tau
                        wsrc = wb1b[q % 2] if he == 0 else wb1[q % 2]
                        mmt[tau] = p.emit("tensor", lambda h, c_h=c_h, tau=tau, c_f=c_f, b=b, q=q, he=he, wsrc=wsrc: h.matmul(
                            bank(b), wsrc[:, c_f * 256 + (c_h % 2) * 128: c_f * 256 + (c_h % 2) * 128 + 128],
                            xn2_fm[:, c_f, tau * 512:(tau + 1) * 512],
                            start=(c_f == 0), stop=(c_f == NCH - 1)),
                            waits=[xn2_tok[tau], tk_w1[(he, q)], bank_tok[b] if c_f == 0 else None],
                            inc=(c_f == NCH - 1))
                bidx = e * 16 + half * 8 + c_h
                for tau in range(NTAU):
                    b = grp + tau
                    tk_hev = p.emit("scalar", lambda h, c_h=c_h, tau=tau, b=b, bidx=bidx: h.activation(
                        Hs[:, c_h, tau * 512:(tau + 1) * 512], bank(b), AF.Relu,
                        bias=b1_s[:, bidx:bidx + 1], scale=1.0),
                        waits=[mmt[tau], tk_cst, dmp.get("afm")], inc=True)
                    bank_tok[b] = tk_hev
                    tk_hs_last = tk_hev
                    hs_tok[c_h] = tk_hev
                    w1_slot_last[q % 2] = tk_hev
                if c_h % 2 == 1:
                    nq = q + 2
                    nhe, nqq = (he, nq) if nq < 4 else (he + 1, nq - 4)
                    if nhe < 16:
                        w1_stream(nhe, nqq, w1_slot_last[nq % 2])
            if he == 0:
                while gate_cl:
                    gate_cl.pop(0)()
                tk_web = emit_web(0)
            for c_o in range(NCH):
                q = c_o // 2
                grp = 4 if c_o % 2 == 0 else 0
                if he == 0:
                    sgrp = 0 if c_o % 2 == 0 else 4
                    seed_toks[c_o] = []
                    for tau in range(NTAU):
                        b = sgrp + tau
                        tk_sm = p.emit("tensor", lambda h, c_o=c_o, tau=tau, b=b: h.matmul(
                            bank(b), b2T[0:8, c_o * 128:(c_o + 1) * 128],
                            W_em[0:8, tau * 512:(tau + 1) * 512],
                            start=True, stop=True),
                            waits=[tk_wem, tk_b2T, bank_tok[b]], inc=True)
                        tk_seed = p.emit("vector", lambda h, c_o=c_o, tau=tau, b=b: h.tensor_copy(
                            o_acc[:, c_o, tau * 512:(tau + 1) * 512], bank(b)),
                            waits=[tk_sm, p.etok("scalar")], inc=True)
                        bank_tok[b] = tk_seed
                        seed_toks[c_o].append(tk_seed)
                mms = None
                for c_h in range(8):
                    for tau in range(NTAU):
                        b = grp + tau
                        mms = p.emit("tensor", lambda h, c_o=c_o, c_h=c_h, tau=tau, b=b, q=q, he=he: h.matmul(
                            bank(b), wb2[q % 2][:, c_h * 256 + (c_o % 2) * 128: c_h * 256 + (c_o % 2) * 128 + 128],
                            Hs[:, c_h, tau * 512:(tau + 1) * 512],
                            start=(c_h == 0), stop=(c_h == 7)),
                            waits=[hs_tok[c_h], tk_w2[(he, q)]] +
                                  ([bank_tok[b]] if c_h == 0 else []),
                            inc=(c_h == 7 and tau == NTAU - 1))
                tk_tmp = p.emit("vector", lambda h, grp=grp: h.tensor_tensor(
                    tmpb[:, :], pw[:, grp * 512:(grp + NTAU) * 512], w_eb[:, 0:NTAU * 512], ALU.mult),
                    waits=[mms, tk_web, tk_oacc], inc=True)
                tk_oacc = p.emit("vector", lambda h, c_o=c_o: h.tensor_add(
                    o_acc[:, c_o, :], o_acc[:, c_o, :], tmpb[:, 0:T]),
                    waits=[tk_tmp, seed_toks[c_o][-1]], inc=True)
                tk_web_war = tk_tmp
                for tau in range(NTAU):
                    bank_tok[grp + tau] = tk_oacc
                w2_slot_last[q % 2] = tk_oacc
                if he == 15:
                    oacc_co_tok[c_o] = tk_oacc
                    if c_o >= 1:
                        emit_final_res(c_o - 1, 4 - grp if grp in (0, 4) else 0)
                if c_o % 2 == 1:
                    nq = q + 2
                    nhe, nqq = (he, nq) if nq < 4 else (he + 1, nq - 4)
                    if nhe < 16:
                        w2_stream(nhe, nqq, w2_slot_last[nq % 2])
        if "ffnacc" in dbg:
            p.dma("gpsimd", "dbg_oacc", dbg_dram("oacc", [128, NCH * T], BF16), o_acc[:, :, :],
                  waits=[tk_oacc])

        emit_final_res(NCH - 1, 0)
        p.dma("sync", "out", y_d, x_tm[:, :], waits=[p.etok("vector")])
        for snm in list(p.dma_sem.keys()):
            if p.dma_cnt[snm]:
                p.emit("sync", lambda h, snm=snm: h.wait_ge(p.dma_sem[snm], p.dma_cnt[snm]))

        p.finalize()
    return nc, dbg_out


def prep_inputs(ins, T=2048):
    NT = T // 128
    o = {}
    x = np.ascontiguousarray(np.asarray(ins["x"]), dtype=np.float32)
    o["x"] = np.ascontiguousarray(x.reshape(NT, 128, 1024).transpose(1, 0, 2).reshape(128, NT * 1024))
    wq = np.ascontiguousarray(np.asarray(ins["Wq"]).T)
    o["wq"] = np.ascontiguousarray(
        wq.reshape(8, 128, 4, 256).transpose(2, 1, 0, 3).reshape(4, 128, 8 * 256))
    wo = np.ascontiguousarray(np.asarray(ins["Wo"]).T)
    o["wo"] = np.ascontiguousarray(
        wo.reshape(8, 128, 4, 256).transpose(2, 1, 0, 3).reshape(4, 128, 8 * 256))
    wkv = np.ascontiguousarray(np.asarray(ins["Wkv"]).T)
    o["wkv"] = np.ascontiguousarray(wkv.reshape(8, 128, 128).transpose(1, 0, 2).reshape(128, 8 * 128))
    o["bq"] = np.ascontiguousarray(np.asarray(ins["bq"]).reshape(8, 128).T)
    o["bo"] = np.ascontiguousarray(np.asarray(ins["bo"]).reshape(8, 128).T)
    o["bkv"] = np.ascontiguousarray(np.asarray(ins["bkv"]).reshape(128, 1))
    gw = np.ascontiguousarray(np.asarray(ins["gate_w"]).T)
    o["gw"] = np.ascontiguousarray(gw.reshape(8, 128, 8).transpose(1, 0, 2).reshape(128, 64))
    o["gb"] = np.ascontiguousarray(np.asarray(ins["gate_b"]).reshape(8, 1))
    b1 = np.asarray(ins["b1"])
    o["b1"] = np.ascontiguousarray(b1.reshape(8, 16, 128).transpose(2, 0, 1).reshape(128, 128))
    o["b2"] = np.ascontiguousarray(np.asarray(ins["b2"]))
    w1 = np.asarray(ins["w1"])
    w1p = np.empty((16, 4, 128, 8 * 256), np.float32)
    for e in range(8):
        for half in range(2):
            for q in range(4):
                blk = w1[e][:, half * 1024 + q * 256: half * 1024 + (q + 1) * 256]
                w1p[e * 2 + half, q] = blk.reshape(8, 128, 256).transpose(1, 0, 2).reshape(128, 8 * 256)
    o["w1"] = w1p
    w2 = np.asarray(ins["w2"])
    w2p = np.empty((16, 4, 128, 8 * 256), np.float32)
    for e in range(8):
        for half in range(2):
            for q in range(4):
                blk = w2[e][half * 1024:(half + 1) * 1024, q * 256:(q + 1) * 256]
                w2p[e * 2 + half, q] = blk.reshape(8, 128, 256).transpose(1, 0, 2).reshape(128, 8 * 256)
    o["w2"] = w2p
    return o


def unprep_output(y, T=2048):
    NT = T // 128
    return np.ascontiguousarray(np.asarray(y).reshape(128, NT, 1024).transpose(1, 0, 2).reshape(T, 1024))


def prep_x(x, T=2048):
    NT = T // 128
    x = np.ascontiguousarray(np.asarray(x), dtype=np.float32)
    return np.ascontiguousarray(x.reshape(NT, 128, 1024).transpose(1, 0, 2).reshape(128, NT * 1024))


def prep_inputs(ins, T=2048):
    NT = T // 128
    o = {}
    x = np.ascontiguousarray(np.asarray(ins["x"]), dtype=np.float32)
    o["x"] = np.ascontiguousarray(x.reshape(NT, 128, 1024).transpose(1, 0, 2).reshape(128, NT * 1024))
    wq = np.ascontiguousarray(np.asarray(ins["Wq"]).T)
    o["wq"] = np.ascontiguousarray(
        wq.reshape(8, 128, 4, 256).transpose(2, 1, 0, 3).reshape(4, 128, 8 * 256))
    wo = np.ascontiguousarray(np.asarray(ins["Wo"]).T)
    o["wo"] = np.ascontiguousarray(
        wo.reshape(8, 128, 4, 256).transpose(2, 1, 0, 3).reshape(4, 128, 8 * 256))
    wkv = np.ascontiguousarray(np.asarray(ins["Wkv"]).T)
    o["wkv"] = np.ascontiguousarray(wkv.reshape(8, 128, 128).transpose(1, 0, 2).reshape(128, 8 * 128))
    o["bq"] = np.ascontiguousarray(np.asarray(ins["bq"]).reshape(8, 128).T)
    o["bo"] = np.ascontiguousarray(np.asarray(ins["bo"]).reshape(8, 128).T)
    o["bkv"] = np.ascontiguousarray(np.asarray(ins["bkv"]).reshape(128, 1))
    gw = np.ascontiguousarray(np.asarray(ins["gate_w"]).T)
    o["gw"] = np.ascontiguousarray(gw.reshape(8, 128, 8).transpose(1, 0, 2).reshape(128, 64))
    o["gb"] = np.ascontiguousarray(np.asarray(ins["gate_b"]).reshape(8, 1))
    b1 = np.asarray(ins["b1"])
    o["b1"] = np.ascontiguousarray(b1.reshape(8, 16, 128).transpose(2, 0, 1).reshape(128, 128))
    o["b2"] = np.ascontiguousarray(np.asarray(ins["b2"]))
    w1 = np.asarray(ins["w1"])
    w1p = np.empty((16, 4, 128, 8 * 256), np.float32)
    for e in range(8):
        for half in range(2):
            for q in range(4):
                blk = w1[e][:, half * 1024 + q * 256: half * 1024 + (q + 1) * 256]
                w1p[e * 2 + half, q] = blk.reshape(8, 128, 256).transpose(1, 0, 2).reshape(128, 8 * 256)
    o["w1"] = w1p
    w2 = np.asarray(ins["w2"])
    w2p = np.empty((16, 4, 128, 8 * 256), np.float32)
    for e in range(8):
        for half in range(2):
            for q in range(4):
                blk = w2[e][half * 1024:(half + 1) * 1024, q * 256:(q + 1) * 256]
                w2p[e * 2 + half, q] = blk.reshape(8, 128, 256).transpose(1, 0, 2).reshape(128, 8 * 256)
    o["w2"] = w2p
    return o


def unprep_output(y, T=2048):
    NT = T // 128
    return np.ascontiguousarray(np.asarray(y).reshape(128, NT, 1024).transpose(1, 0, 2).reshape(T, 1024))

from concourse.bass_utils import run_bass_kernel_spmd

_T = 2048


def kernel(L0_attn_x, L0_attn_Wq, L0_attn_bq, L0_attn_Wkv, L0_attn_bkv,
           L0_attn_Wo, L0_attn_bo, L0_ffn_gate_w, L0_ffn_gate_b,
           L0_ffn_w1, L0_ffn_b1, L0_ffn_w2, L0_ffn_b2):
    x = np.ascontiguousarray(np.asarray(L0_attn_x, dtype=np.float32))
    B = x.shape[0]
    shared = {
        "Wq": L0_attn_Wq, "bq": L0_attn_bq, "Wkv": L0_attn_Wkv, "bkv": L0_attn_bkv,
        "Wo": L0_attn_Wo, "bo": L0_attn_bo, "gate_w": L0_ffn_gate_w,
        "gate_b": L0_ffn_gate_b, "w1": L0_ffn_w1, "b1": L0_ffn_b1,
        "w2": L0_ffn_w2, "b2": L0_ffn_b2,
    }
    base = prep_inputs(dict(shared, x=x[0]), T=_T)
    in_maps = []
    for b in range(B):
        m = dict(base)
        m["x"] = prep_x(x[b], T=_T)
        in_maps.append(m)
    nc, _ = build(T=_T)
    res = run_bass_kernel_spmd(nc, in_maps, core_ids=list(range(B)))
    out = np.empty((B, _T, 1024), np.float32)
    for b in range(B):
        out[b] = unprep_output(np.asarray(res.results[b]["y"]), T=_T)
    return out


# version 21 (final)
# speedup vs baseline: 1.0341x; 1.0341x over previous
import contextlib
import numpy as np
import concourse.bass as bass
import concourse.mybir as mybir

F32 = mybir.dt.float32
BF16 = mybir.dt.bfloat16
AF = mybir.ActivationFunctionType
ALU = mybir.AluOpType
AX = mybir.AxisListType

ENGINES = ("tensor", "vector", "scalar", "gpsimd", "sync")


class Prog:
    def __init__(self, nc, stack, dma_streams=()):
        self.nc = nc
        self.stack = stack
        self.ops = {e: [] for e in ENGINES}
        self.sem = {e: stack.enter_context(nc.semaphore(f"sem_{e}")) for e in ENGINES}
        self.cnt = {e: 0 for e in ENGINES}
        self.dma_sem = {}
        self.dma_cnt = {}
        for name in dma_streams:
            self.dma_sem[name] = stack.enter_context(nc.semaphore(f"d_{name}"))
            self.dma_cnt[name] = 0
        self._waited = {e: {} for e in ENGINES}

    def _wait(self, eng, tok):
        if tok is None:
            return
        sem, val = tok
        key = sem.name if hasattr(sem, "name") else id(sem)
        if self._waited[eng].get(key, -1) >= val:
            return
        self._waited[eng][key] = val
        self.ops[eng].append(lambda h, sem=sem, val=val: h.wait_ge(sem, val))

    def emit(self, eng, fn, waits=(), inc=False):
        for t in waits:
            self._wait(eng, t)
        if inc:
            self.cnt[eng] += 1
            v = self.cnt[eng]
            sem = self.sem[eng]

            def run(h, fn=fn, sem=sem):
                fn(h).then_inc(sem, 1)

            self.ops[eng].append(run)
            return (sem, v)
        self.ops[eng].append(lambda h, fn=fn: fn(h))
        return None

    def dma(self, q, stream, out, in_, waits=(), **kw):
        if stream not in self.dma_sem:
            self.dma_sem[stream] = self.stack.enter_context(self.nc.semaphore(f"d_{stream}"))
            self.dma_cnt[stream] = 0
        for t in waits:
            self._wait(q, t)
        self.dma_cnt[stream] += 16
        v = self.dma_cnt[stream]
        sem = self.dma_sem[stream]
        self.ops[q].append(
            lambda h, out=out, in_=in_, sem=sem: h.dma_start(out=out, in_=in_, **kw).then_inc(sem, 16)
        )
        return (sem, v)

    def etok(self, eng):
        return (self.sem[eng], self.cnt[eng])

    def finalize(self):
        with self.nc.Block() as block:
            for name, deco in (("sync", block.sync), ("gpsimd", block.gpsimd),
                               ("scalar", block.scalar), ("vector", block.vector),
                               ("tensor", block.tensor)):
                if self.ops[name]:
                    ops = self.ops[name]

                    def runner(h, ops=ops):
                        for f in ops:
                            f(h)
                    deco(runner)


def build(T=2048, dbg=()):
    NT = T // 128
    SC = NT
    NTAU = T // 512
    HB = T // 16
    UB = 512 // HB
    NCH = 8
    dbg = set(dbg)
    assert NTAU <= 4

    nc = bass.Bass("TRN2", target_bir_lowering=False)

    def dram_in(name, shape):
        return nc.dram_tensor(name, shape, F32, kind="ExternalInput").ap()

    x_d = dram_in("x", [128, NT * 1024])
    wq_d = dram_in("wq", [4, 128, 8 * 256])
    wo_d = dram_in("wo", [4, 128, 8 * 256])
    wkv_d = dram_in("wkv", [128, 8 * 128])
    bq_d = dram_in("bq", [128, 8])
    bo_d = dram_in("bo", [128, 8])
    bkv_d = dram_in("bkv", [128, 1])
    gw_d = dram_in("gw", [128, 64])
    gb_d = dram_in("gb", [8, 1])
    b1_d = dram_in("b1", [128, 128])
    b2_d = dram_in("b2", [8, 1024])
    w1_d = dram_in("w1", [16, 4, 128, 8 * 256])
    w2_d = dram_in("w2", [16, 4, 128, 8 * 256])
    y_d = nc.dram_tensor("y", [128, NT * 1024], F32, kind="ExternalOutput").ap()

    dbg_out = {}
    def dbg_dram(name, shape, dtype=F32):
        dbg_out[name] = nc.dram_tensor("dbg_" + name, shape, dtype, kind="ExternalOutput").ap()
        return dbg_out[name]

    with contextlib.ExitStack() as stack:
        OFF = [16768]
        def sb(name, shape, dtype=F32, at=None):
            nbytes = int(np.prod(shape[1:])) * mybir.dt.size(dtype)
            if at is None:
                at = OFF[0]
            OFF[0] = max(OFF[0], at + ((nbytes + 31) // 32) * 32)
            return nc.alloc_sbuf_tensor_at(name, list(shape), dtype, offset=at)

        x_tm = sb("x_tm", [128, NT * 1024], F32)
        r1 = OFF[0]
        xn1_fm = sb("xn1_fm", [128, NCH, T], BF16)
        A_fm = sb("A_fm", [128, NCH, T], BF16, at=r1)
        Hs = sb("Hs", [128, NCH, T], BF16, at=r1)
        r2 = OFF[0]
        q_fm = sb("q_fm", [128, NCH, T], BF16)
        xn2_fm = sb("xn2_fm", [128, NCH, T], BF16, at=r2)
        r3 = OFF[0]
        PT_A = sb("PT_A", [128, SC * 512], BF16)
        PT_B = sb("PT_B", [128, SC * 512], BF16)
        o_acc = sb("o_acc", [128, NCH, T], BF16, at=r3)
        qsq = sb("qsq", [128, 512], BF16, at=r3)
        qsq2 = sb("qsq2", [128, 512], BF16, at=r3 + 1024)
        sqd = sb("sqd", [128, 1024], BF16, at=r3 + 2048)
        s_big = sb("s_big", [128, 2 * NTAU * 512], BF16, at=r3 + 4096)
        woq23 = sb("woq23", [128, 2 * 2048], BF16, at=r3 + 12288)
        wb1b = [sb("wb1b0", [128, 8 * 256], BF16, at=r3 + 20480),
                sb("wb1b1", [128, 8 * 256], BF16, at=r3 + 24576)]
        r4 = OFF[0]
        wqs = [sb("wqs0", [128, 8 * 256], BF16), sb("wqs1", [128, 8 * 256], BF16)]
        wos = [sb("wos0", [128, 8 * 256], BF16, at=r4),
               sb("wos1", [128, 8 * 256], BF16, at=r4 + 4096)]
        wb1 = [sb("wb1_0", [128, 8 * 256], BF16, at=r4),
               sb("wb1_1", [128, 8 * 256], BF16, at=r4 + 4096)]
        r5 = OFF[0]
        kv_fm = sb("kv_fm", [128, T], BF16)
        X0 = OFF[0]
        khi = sb("khi", [128, T], BF16, at=X0)
        O_st = sb("O_st", [64, 16 * 512], BF16, at=X0)
        wb2 = [sb("wb2_0", [128, 8 * 256], BF16, at=X0),
               sb("wb2_1", [128, 8 * 256], BF16, at=X0 + 4096)]
        O_hi = sb("O_hi", [128, 16 * 256], BF16, at=X0 + 4096)
        w_eb = sb("w_eb", [128, T], BF16, at=X0 + 8192)
        tmpb = sb("tmpb", [128, T], BF16, at=X0 + 8192 + 2 * T)
        OFF[0] = max(OFF[0], X0 + 16384, X0 + 8192 + 4 * T)
        v_aug = sb("v_aug", [128, SC * 65], BF16)
        wkvs = sb("wkvs", [128, 8 * 128], BF16, at=OFF[0] - ((SC * 65 * 2 + 31) // 32) * 32)
        xns_base = OFF[0]
        xns = [sb("xns0", [128, 1024], BF16), sb("xns1", [128, 1024], BF16)]
        W_em = sb("W_em", [8, T], BF16, at=xns_base)
        rs = OFF[0]
        zr_st = sb("zr_st", [65, 512], BF16)
        zinv32 = sb("zinv32", [16, 512], F32)
        e16t = sb("e16t", [16, 16 * 128], BF16)
        attn_end = OFF[0]
        OFF[0] = rs
        Gtm = sb("Gtm", [128, NT * 8], F32)
        probs = sb("probs", [128, NT * 8], F32)
        Wtm = sb("Wtm", [128, NT * 8], BF16)
        b2T = sb("b2T", [8, 1024], BF16)
        e8 = sb("e8", [8, 8 * 128], BF16)
        OFF[0] = max(OFF[0], attn_end)
        gm1 = sb("gm1", [128, NT], F32)
        gm1n = sb("gm1n", [128, NT], F32)
        gm2 = sb("gm2", [128, NT], F32)
        gtmp = sb("gtmp", [128, NT * 8], F32)
        G_sb = sb("G_sb", [8, 256], F32)
        ident = sb("ident", [128, 128], BF16)
        identf = sb("identf", [8, 8], F32)
        onesb = sb("onesb", [128, 128], BF16)
        zsel16 = sb("zsel16", [65, 256], BF16)
        qsel = sb("qsel", [128, 16], BF16)
        stt4 = sb("stt4", [4, 1], F32)
        gws = sb("gws", [128, 64], BF16)
        bq_s = sb("bq_s", [128, 8], F32)
        bo_s = sb("bo_s", [128, 8], F32)
        bkv_s = sb("bkv_s", [128, 1], F32)
        gb_s = sb("gb_s", [8, 1], F32)
        b1_s = sb("b1_s", [128, 128], F32)
        sums = sb("sums", [128, NT], F32)
        sqs = sb("sqs", [128, NT], F32)
        mean = sb("mean", [128, NT], F32)
        inv = sb("inv", [128, NT], F32)

        stt1 = sb("stt1", [1, 8], F32)
        negb = sb("negb", [1, 1], BF16)
        negB_b = sb("negB_b", [128, 1], F32)

        used = OFF[0]
        assert used <= 229376, f"SBUF over budget: {used}"

        pw = stack.enter_context(nc.psum_tensor("pw", [128, 8 * 512], F32))
        pwb = pw.bitcast(BF16)

        def bank(i, lo=0, hi=128, w=512):
            return pw[lo:hi, i * 512:i * 512 + w]

        def bankb(i, lo=0, hi=128, w=1024):
            return pwb[lo:hi, i * 1024:i * 1024 + w]

        streams = ["cst", "sml", "wqa", "wqb", "woa", "wob",
                   "w1a", "w1b", "w2a", "w2b", "kvcp", "out"]
        p = Prog(nc, stack, dma_streams=streams)
        dmp = {}
        qstat_mm = {}
        qstat_bank = {}
        sq_slot_tok = [None, None]
        sq_slot_wr = [None, None]
        bank_tok = [None] * 8

        CH4 = NT * 1024 // 4
        tk_xc = [p.dma("sync", f"x{i}", x_tm[:, i * CH4:(i + 1) * CH4],
                       x_d[:, i * CH4:(i + 1) * CH4]) for i in range(4)]
        for dst, src in ((bq_s, bq_d), (bo_s, bo_d), (bkv_s, bkv_d), (gb_s, gb_d), (b1_s, b1_d)):
            p.dma("sync", "cst", dst[:, :], src)
        tk_cst = (p.dma_sem["cst"], p.dma_cnt["cst"])
        p.dma("gpsimd", "sml", wkvs[:, :], wkv_d)
        p.dma("gpsimd", "sml", gws[:, :], gw_d)
        tk_sml = (p.dma_sem["sml"], p.dma_cnt["sml"])

        tk_ones = p.emit("gpsimd", lambda h: h.memset(onesb[:, :], 1.0), inc=True)
        tk_id = p.emit("gpsimd", lambda h: h.affine_select(
            out=ident[:, :], in_=onesb[:, :], pattern=[[1, 128]],
            compare_op=ALU.is_equal, fill=0.0, base=0, channel_multiplier=-1),
            waits=[tk_ones], inc=True)
        tk_if0 = p.emit("gpsimd", lambda h: h.memset(identf[:, :], 1.0), inc=True)
        tk_idf = p.emit("gpsimd", lambda h: h.affine_select(
            out=identf[:, :], in_=identf[:, :], pattern=[[1, 8]],
            compare_op=ALU.is_equal, fill=0.0, base=0, channel_multiplier=-1),
            waits=[tk_if0], inc=True)
        tk_qs0 = p.emit("gpsimd", lambda h: h.memset(qsel[:, :], 1.0), inc=True)
        tk_qsel = p.emit("gpsimd", lambda h: h.affine_select(
            out=qsel[:, :], in_=qsel[:, :], pattern=[[1, 4], [-1, 4]],
            compare_op=ALU.is_equal, fill=0.0, base=0, channel_multiplier=0),
            waits=[tk_qs0], inc=True)
        tk_z0 = p.emit("gpsimd", lambda h: h.memset(zsel16[:, :], 1.0), inc=True)
        tk_zsel = p.emit("gpsimd", lambda h: h.affine_select(
            out=zsel16[:, :], in_=zsel16[:, :], pattern=[[1, 16], [-1, 16]],
            compare_op=ALU.is_equal, fill=0.0, base=0, channel_multiplier=0),
            waits=[tk_z0], inc=True)
        tk_e0 = p.emit("gpsimd", lambda h: h.memset(e16t[:, :], 1.0), inc=True)
        tk_e16 = p.emit("gpsimd", lambda h: h.affine_select(
            out=e16t[:, :], in_=e16t[:, :], pattern=[[1, 16], [0, 128]],
            compare_op=ALU.is_equal, fill=0.0, base=0, channel_multiplier=-1),
            waits=[tk_e0], inc=True)
        tk_bq = p.emit("vector", lambda h: h.tensor_scalar_mul(bq_s[:, :], bq_s[:, :], 0.125),
                       waits=[tk_cst], inc=True)

        tk_wq = [None] * 4
        tk_wq[0] = p.dma("gpsimd", "wqa", wqs[0][:, :], wq_d[0])
        tk_wq[1] = p.dma("gpsimd", "wqb", wqs[1][:, :], wq_d[1])

        def ln(xn_dst, x_tok, sq_war=None, x_toks=None):
            TPG = max(NT // NTAU, 1)
            xt = (lambda i: x_toks[min(i * 4 // NT, len(x_toks) - 1)] if len(x_toks) == 4
                  else x_toks[i]) if x_toks else (lambda i: x_tok)
            sum_tok = [None] * NTAU
            sq_tok = [None] * NTAU
            inv_tok = [None] * NTAU
            tk_sq = sq_war
            for g in range(NTAU):
                for i in range(g * TPG, (g + 1) * TPG):
                    sum_tok[g] = p.emit("vector", lambda h, i=i: h.reduce_sum(
                        sums[:, i:i + 1], x_tm[:, i * 1024:(i + 1) * 1024], AX.X),
                        waits=[xt(i)], inc=(i == (g + 1) * TPG - 1))
                for i in range(g * TPG, (g + 1) * TPG):
                    tk_sq = p.emit("scalar", lambda h, i=i: h.activation(
                        sqd[:, :], x_tm[:, i * 1024:(i + 1) * 1024], AF.Square,
                        accum_out=sqs[:, i:i + 1]),
                        waits=[xt(i), tk_sq], inc=True)
                sq_tok[g] = tk_sq
                sl = slice(g * TPG, (g + 1) * TPG)
                tk_mean = p.emit("vector", lambda h, sl=sl: h.tensor_scalar_mul(
                    mean[:, sl], sums[:, sl], 1.0 / 1024),
                    waits=[sum_tok[g]], inc=True)
                tk_m2 = p.emit("vector", lambda h, sl=sl: h.tensor_mul(
                    inv[:, sl], mean[:, sl], mean[:, sl]), waits=[tk_mean], inc=True)
                tk_var = p.emit("vector", lambda h, sl=sl: h.scalar_tensor_tensor(
                    inv[:, sl], sqs[:, sl], 1.0 / 1024, inv[:, sl], ALU.mult, ALU.subtract),
                    waits=[tk_m2, sq_tok[g]], inc=True)
                tk_eps = p.emit("vector", lambda h, sl=sl: h.tensor_scalar_add(
                    inv[:, sl], inv[:, sl], 1e-5), waits=[tk_var], inc=True)
                tk_std = p.emit("scalar", lambda h, sl=sl: h.activation(
                    inv[:, sl], inv[:, sl], AF.Sqrt), waits=[tk_eps], inc=True)
                inv_tok[g] = p.emit("vector", lambda h, sl=sl: h.reciprocal(
                    inv[:, sl], inv[:, sl]), waits=[tk_std], inc=True)
            touts = [None] * NT
            for i in range(NT):
                g = i // TPG
                slot = xns[i % 2]
                b = i % 2
                tk_xn = p.emit("vector", lambda h, i=i, slot=slot: h.tensor_scalar(
                    slot[:, :], x_tm[:, i * 1024:(i + 1) * 1024],
                    mean[:, i:i + 1], inv[:, i:i + 1], ALU.subtract, ALU.mult),
                    waits=[inv_tok[g], bank_tok[i % 2], xt(i)], inc=True)
                tts = None
                for c in range(NCH):
                    tts = p.emit("tensor", lambda h, c=c, b=b, slot=slot: h.transpose(
                        bankb(b)[:, c * 128:(c + 1) * 128],
                        slot[:, c * 128:(c + 1) * 128], ident[:, :]),
                        waits=[tk_xn, tk_id, bank_tok[b]], inc=(c == NCH - 1))
                touts[i] = p.emit("vector", lambda h, i=i, b=b: h.tensor_copy(
                    xn_dst[:, :, i * 128:(i + 1) * 128],
                    bankb(b)[:, :].rearrange("p (c k) -> p c k", k=128)),
                    waits=[tts], inc=True)
                bank_tok[b] = touts[i]
            return touts

        tk_xn1 = ln(xn1_fm, tk_xc[-1], x_toks=tk_xc)[-1]
        if "xn1" in dbg:
            dmp["r1"] = p.dma("gpsimd", "dbg_xn1", dbg_dram("xn1", [128, NCH * T], BF16),
                              xn1_fm[:, :, :], waits=[tk_xn1])

        tk_q_evac = None
        wq_slot_last = [None, None]
        qstat_bank_all = bank(5, lo=0, hi=4)
        for c_o in range(NCH):
            qq = c_o // 2
            mmt = [None] * NTAU
            for c_e in range(NCH):
                for tau in range(NTAU):
                    b = tau % 4
                    mmt[tau] = p.emit("tensor", lambda h, c_o=c_o, tau=tau, c_e=c_e, b=b, qq=qq: h.matmul(
                        bank(b), wqs[qq % 2][:, c_e * 256 + (c_o % 2) * 128: c_e * 256 + (c_o % 2) * 128 + 128],
                        xn1_fm[:, c_e, tau * 512:(tau + 1) * 512],
                        start=(c_e == 0), stop=(c_e == NCH - 1)),
                        waits=[tk_xn1, tk_wq[qq], bank_tok[b] if c_e == 0 else None],
                        inc=(c_e == NCH - 1))
            for tau in range(NTAU):
                b = tau % 4
                tk_q_evac = p.emit("scalar", lambda h, c_o=c_o, tau=tau, b=b: h.activation(
                    q_fm[:, c_o, tau * 512:(tau + 1) * 512], bank(b), AF.Identity,
                    bias=bq_s[:, c_o:c_o + 1], scale=0.125),
                    waits=[mmt[tau], tk_bq], inc=True)
                bank_tok[b] = tk_q_evac
                wq_slot_last[qq % 2] = tk_q_evac
                slot = qsq if (c_o % 2 == 0) else qsq2
                tk_sqc = p.emit("scalar", lambda h, c_o=c_o, tau=tau, slot=slot: h.activation(
                    slot[:, :], q_fm[:, c_o, tau * 512:(tau + 1) * 512], AF.Square),
                    waits=[tk_q_evac, sq_slot_tok[c_o % 2], sq_slot_wr[c_o % 2]], inc=True)
                sq_slot_wr[c_o % 2] = tk_sqc
                qstat_mm[tau] = p.emit("tensor", lambda h, c_o=c_o, tau=tau, slot=slot: h.matmul(
                    qstat_bank_all, qsel[:, (tau % 4) * 4:(tau % 4) * 4 + 4], slot[:, :],
                    start=(c_o == 0 and tau == 0), stop=(c_o == NCH - 1 and tau == NTAU - 1)),
                    waits=[tk_sqc, tk_qsel] + ([bank_tok[5]] if c_o == 0 and tau == 0 else []),
                    inc=True)
                sq_slot_tok[c_o % 2] = qstat_mm[tau]
            if c_o % 2 == 1:
                nq = qq + 2
                if nq < 4:
                    tk_wq[nq] = p.dma("gpsimd", "wqa" if nq % 2 == 0 else "wqb",
                                      wqs[nq % 2][:, :], wq_d[nq], waits=[wq_slot_last[nq % 2]])

        tk_wo = [None] * 4
        tk_wo[0] = p.dma("gpsimd", "woa", wos[0][:, :], wo_d[0], waits=[p.etok("tensor")])
        tk_wo[1] = p.dma("gpsimd", "wob", wos[1][:, :], wo_d[1])

        tk_kv_evac = None
        mmt = [None] * NTAU
        for c_e in range(NCH):
            for tau in range(NTAU):
                b = tau % 4
                mmt[tau] = p.emit("tensor", lambda h, tau=tau, c_e=c_e, b=b: h.matmul(
                    bank(b), wkvs[:, c_e * 128:(c_e + 1) * 128],
                    xn1_fm[:, c_e, tau * 512:(tau + 1) * 512],
                    start=(c_e == 0), stop=(c_e == NCH - 1)),
                    waits=[tk_xn1, tk_sml, bank_tok[b] if c_e == 0 else None],
                    inc=(c_e == NCH - 1))
        for tau in range(NTAU):
            b = tau % 4
            tk_kv_evac = p.emit("scalar", lambda h, tau=tau, b=b: h.activation(
                kv_fm[:, tau * 512:(tau + 1) * 512], bank(b), AF.Identity,
                bias=bkv_s[:, 0:1], scale=1.0),
                waits=[mmt[tau], tk_cst], inc=True)
            bank_tok[b] = tk_kv_evac

        tk_vones = p.emit("gpsimd", lambda h: h.memset(
            v_aug[:, :].rearrange("p (j q) -> p j q", q=65)[:, :, 64:65], 1.0),
            waits=[p.etok("tensor")], inc=True)
        tk_kvx2 = p.dma("sync", "kvcp", khi[64:128, :], kv_fm[0:64, :], waits=[tk_kv_evac])
        tk_kvx = (p.dma_sem["kvcp"], p.dma_cnt["kvcp"])

        vts = None
        for j in range(SC):
            vts = p.emit("tensor", lambda h, j=j: h.transpose(
                bankb(4)[:, j * 64:(j + 1) * 64],
                kv_fm[64:128, j * 128:(j + 1) * 128], ident[64:128, 64:128]),
                waits=[tk_kv_evac, tk_id, bank_tok[4]], inc=(j == SC - 1))
        tk_vaug = p.emit("vector", lambda h: h.tensor_copy(
            v_aug[:, :].rearrange("p (j q) -> p j q", q=65)[:, :, 0:64],
            bankb(4, w=SC * 64)[:, :].rearrange("p (j q) -> p j q", q=64)),
            waits=[vts, tk_vones], inc=True)
        bank_tok[4] = tk_vaug

        tk_qmax = tk_kmax = None
        for which in range(2):
            src_is_q = (which == 0)
            if src_is_q:
                tk_r = p.emit("vector", lambda h: h.reduce_max(
                    stt4[0:4, 0:1], bank(5, lo=0, hi=4), AX.X),
                    waits=[qstat_mm[NTAU - 1]], inc=True)
                bank_tok[5] = tk_r
                tk_tr = p.emit("tensor", lambda h: h.transpose(
                    bank(6, lo=0, hi=1, w=4), stt4[0:4, 0:1], identf[0:4, 0:4]),
                    waits=[tk_r, tk_idf, bank_tok[6]], inc=True)
                tk_qmax = p.emit("vector", lambda h: h.reduce_max(
                    stt1[0:1, 0:1], bank(6, lo=0, hi=1, w=4), AX.X),
                    waits=[tk_tr], inc=True)
                bank_tok[6] = tk_qmax
                continue
            nchunk = 1
            first = True
            tk_r = None
            for tau in range(NTAU):
                mms = None
                for c in range(nchunk):
                    slot = qsq if (c % 2 == 0) else qsq2
                    tk_sqc = p.emit("scalar", lambda h, c=c, tau=tau, slot=slot: h.activation(
                        slot[:, :], kv_fm[:, tau * 512:(tau + 1) * 512], AF.Square),
                        waits=[tk_kv_evac, sq_slot_tok[c % 2], sq_slot_wr[c % 2]], inc=True)
                    sq_slot_wr[c % 2] = tk_sqc
                    mms = p.emit("tensor", lambda h, c=c, nchunk=nchunk, slot=slot: h.matmul(
                        bank(5, lo=0, hi=1), onesb[:, 0:1], slot[:, :],
                        start=(c == 0), stop=(c == nchunk - 1)),
                        waits=[tk_sqc, tk_ones, bank_tok[5]], inc=True)
                    sq_slot_tok[c % 2] = mms
                dst = which if first else which + 2
                tk_r = p.emit("vector", lambda h, dst=dst: h.reduce_max(
                    stt1[0:1, dst:dst + 1], bank(5, lo=0, hi=1), AX.X),
                    waits=[mms, tk_r], inc=True)
                bank_tok[5] = tk_r
                if not first:
                    tk_r = p.emit("vector", lambda h, which=which: h.tensor_max(
                        stt1[0:1, which:which + 1], stt1[0:1, which:which + 1],
                        stt1[0:1, which + 2:which + 3]), waits=[tk_r], inc=True)
                first = False
            tk_kmax = tk_r
        tk_b2v = p.emit("vector", lambda h: h.tensor_mul(stt1[0:1, 4:5], stt1[0:1, 0:1], stt1[0:1, 1:2]),
                        waits=[tk_qmax, tk_kmax], inc=True)
        tk_bsq = p.emit("scalar", lambda h: h.activation(stt1[0:1, 5:6], stt1[0:1, 4:5], AF.Sqrt),
                        waits=[tk_b2v], inc=True)
        tk_nb = p.emit("vector", lambda h: h.tensor_copy(negb[0:1, 0:1], stt1[0:1, 5:6]),
                       waits=[tk_bsq], inc=True)
        tk_nbmm = p.emit("tensor", lambda h: h.matmul(bank(5, lo=0, hi=128, w=1), onesb[0:1, :],
                                                      negb[0:1, 0:1], start=True, stop=True),
                         waits=[tk_nb, bank_tok[5]], inc=True)
        tk_negB = p.emit("scalar", lambda h: h.activation(negB_b[:, :], bank(5, w=1), AF.Copy,
                                                          bias=0.0, scale=-1.0),
                         waits=[tk_nbmm], inc=True)
        bank_tok[5] = tk_negB

        if "qkv" in dbg:
            dmp["q"] = p.dma("gpsimd", "dbg_q", dbg_dram("q", [128, NCH * T], BF16), q_fm[:, :, :], waits=[tk_q_evac])
            dmp["kv"] = p.dma("gpsimd", "dbg_kv", dbg_dram("kv", [128, T], BF16), kv_fm[:, :], waits=[tk_kv_evac])
            dmp["vaug"] = p.dma("gpsimd", "dbg_vaug", dbg_dram("vaug", [128, SC * 65], BF16), v_aug[:, :], waits=[tk_vaug])
            p.dma("gpsimd", "dbg_negB", dbg_dram("negB", [128, 1]), negB_b[:, :], waits=[tk_negB])

        ost_tok = [None] * 16
        ohi_toks = [None] * 16
        zr_hi_tok = {32: None, 64: None}
        zr_lo_tok = None
        zq_bank_tok = None
        sc2_tok = None
        pv_bank_tok = {4: bank_tok[4], 5: bank_tok[5]}
        zq_bank_tok = bank_tok[6]
        dbg_pt_done = [False]

        pv_prev = [None, None]
        scat_queue = []

        def make_scatter(tau, ch, oev_tok, ohi_tok, zib_tok, zbk=7):
            def emit_scatter():
                nonlocal zr_lo_tok
                tk_zb = p.emit("tensor", lambda h, ch=ch, zbk=zbk: h.matmul(
                    bank(zbk), e16t[:, ch * 128:(ch + 1) * 128], zr_st[0:16, 0:512],
                    start=True, stop=True),
                    waits=[zib_tok, tk_e16, bank_tok[zbk]], inc=True)
                zr_lo_tok = tk_zb
                last_rd = None
                UH = UB // 2
                cpr0 = (tau * UB) >> 1
                dstE = A_fm[0:64, :, :].rearrange("p c (k s) -> p c s k", s=16)[
                    :, cpr0:cpr0 + UH, ch, :]
                dstO = A_fm[64:128, :, :].rearrange("p c (k s) -> p c s k", s=16)[
                    :, cpr0:cpr0 + UH, ch, :]
                oev_e = O_st[0:64, ch * 512:(ch + 1) * 512].rearrange(
                    "p (uu k) -> p uu k", k=HB)[:, 0::2, :]
                ohi_o = O_hi[64:128, ch * 256:(ch + 1) * 256].rearrange(
                    "p (uu k) -> p uu k", k=HB)
                zbE = bank(zbk, lo=0, hi=64)[:, :].rearrange(
                    "p (uu k) -> p uu k", k=HB)[:, 0::2, :]
                zbO = bank(zbk, lo=64, hi=128)[:, :].rearrange(
                    "p (uu k) -> p uu k", k=HB)[:, 1::2, :]
                p.emit("vector", lambda h: h.tensor_tensor(dstE, oev_e, zbE, ALU.mult),
                       waits=[tk_zb, oev_tok, dmp.get("r1")], inc=False)
                last_rd = p.emit("vector", lambda h: h.tensor_tensor(dstO, ohi_o, zbO, ALU.mult),
                                 waits=[tk_zb, ohi_tok, dmp.get("r1")], inc=True)
                ost_tok[ch] = last_rd
                bank_tok[zbk] = last_rd
            return emit_scatter

        zbs_war = [None]
        for tau in range(NTAU):
            oev_toks = {}
            zq_last = None
            for c in range(8):
                if scat_queue:
                    scat_queue.pop(0)()
                expA = [None] * SC
                expB = [None] * SC
                pvA = pvB = None
                for j0 in range(0, SC + 2, 2):
                    if j0 == (SC // 2) and scat_queue:
                        scat_queue.pop(0)()
                    for j in (j0, j0 + 1):
                        if j < SC:
                            bA = j % 2
                            bB = 2 + j % 2
                            tkp = p.emit("tensor", lambda h, j=j, tau=tau, c=c, bA=bA: h.matmul(
                                bank(bA), kv_fm[0:64, j * 128:(j + 1) * 128],
                                q_fm[0:64, c, tau * 512:(tau + 1) * 512],
                                start=True, stop=True, tile_position=(0, 0)),
                                waits=[tk_kv_evac, tk_q_evac, bank_tok[bA]], inc=True)
                            tkA2 = tkp
                            if j % 2 == 1:
                                qkA = tkp
                            tkp = p.emit("tensor", lambda h, j=j, tau=tau, c=c, bB=bB: h.matmul(
                                bank(bB), khi[64:128, j * 128:(j + 1) * 128],
                                q_fm[64:128, c, tau * 512:(tau + 1) * 512],
                                start=True, stop=True, tile_position=(64, 0)),
                                waits=[tk_kvx, bank_tok[bB]], inc=True)
                            if j % 2 == 1:
                                qkB = tkp
                            eA = p.emit("scalar", lambda h, j=j, bA=bA: h.activation(
                                PT_A[:, j * 512:(j + 1) * 512], bank(bA), AF.Exp,
                                bias=negB_b[:, 0:1], scale=1.0),
                                waits=[tkA2, tk_negB, pv_prev[0], dmp.get("pt")], inc=True)
                            expA[j] = eA
                            eB = p.emit("scalar", lambda h, j=j, bB=bB: h.activation(
                                PT_B[:, j * 512:(j + 1) * 512], bank(bB), AF.Exp,
                                bias=negB_b[:, 0:1], scale=1.0),
                                waits=[tkp, pv_prev[1], dmp.get("pt")], inc=True)
                            expB[j] = eB
                            bank_tok[bA] = eA
                            bank_tok[bB] = eB
                    for j in (j0 - 2, j0 - 1):
                        if 0 <= j < SC:
                            jj = j
                            pvA = p.emit("tensor", lambda h, jj=jj: h.matmul(
                                bank(4, lo=0, hi=65), v_aug[:, jj * 65:(jj + 1) * 65],
                                PT_A[:, jj * 512:(jj + 1) * 512],
                                start=(jj == 0), stop=(jj == SC - 1)),
                                waits=[tk_vaug, expA[jj], pv_bank_tok[4] if jj == 0 else None],
                                inc=(jj == SC - 1))
                    for j in (j0 - 2, j0 - 1):
                        if 0 <= j < SC:
                            jj = j
                            pvB = p.emit("tensor", lambda h, jj=jj: h.matmul(
                                bank(5, lo=0, hi=65), v_aug[:, jj * 65:(jj + 1) * 65],
                                PT_B[:, jj * 512:(jj + 1) * 512],
                                start=(jj == 0), stop=(jj == SC - 1)),
                                waits=[expB[jj], pv_bank_tok[5] if jj == 0 else None],
                                inc=(jj == SC - 1))
                if "pt" in dbg and tau == 0 and c == 0 and not dbg_pt_done[0]:
                    dbg_pt_done[0] = True
                    dmp["pt"] = p.dma("gpsimd", "dbg_ptA", dbg_dram("ptA", [128, SC * 512], BF16),
                                      PT_A[:, :], waits=[pvA])
                pv_prev = [pvA, pvB]
                for ch, tk_pv, bpv in ((2 * c, pvA, 4), (2 * c + 1, pvB, 5)):
                    zrow = 64 if ch % 2 == 0 else 32
                    tk_oev = p.emit("vector", lambda h, ch=ch, bpv=bpv: h.tensor_copy(
                        O_st[0:64, ch * 512:ch * 512 + 512], bank(bpv, lo=0, hi=64)),
                        waits=[tk_pv, ost_tok[ch]], inc=True)
                    oev_toks[ch] = tk_oev
                    if UB >= 2:
                        ohi_toks[ch] = p.dma("sync", f"oh{ch}",
                            O_hi[64:128, ch * 256:(ch + 1) * 256].rearrange(
                                "p (uu k) -> p uu k", k=HB),
                            O_st[0:64, ch * 512:(ch + 1) * 512].rearrange(
                                "p (uu k) -> p uu k", k=HB)[:, 1::2, :],
                            waits=[tk_oev, ost_tok[ch]])
                    tk_zev = p.emit("vector", lambda h, bpv=bpv, zrow=zrow: h.tensor_copy(
                        zr_st[zrow:zrow + 1, :], bank(bpv, lo=64, hi=65)),
                        waits=[tk_pv, zr_hi_tok[zrow]], inc=True)
                    tk_zq = p.emit("tensor", lambda h, ch=ch, zrow=zrow: h.matmul(
                        bank(6, lo=0, hi=16), zsel16[zrow:zrow + 1, ch * 16:(ch + 1) * 16],
                        zr_st[zrow:zrow + 1, :],
                        start=(ch == 0), stop=(ch == 15)),
                        waits=[tk_zev, tk_zsel] + ([zq_bank_tok] if ch == 0 else []),
                        inc=True)
                    zr_hi_tok[zrow] = tk_zq
                    zq_last = tk_zq
                    pv_bank_tok[bpv] = tk_zev
            tk_zi = p.emit("vector", lambda h: h.reciprocal(zinv32[:, :], bank(6, lo=0, hi=16)),
                           waits=[zq_last, dmp.get("zq")], inc=True)
            zq_bank_tok = tk_zi
            tk_zib = p.emit("vector", lambda h: h.tensor_copy(zr_st[0:16, 0:512], zinv32[:, :]),
                            waits=[tk_zi, zr_lo_tok], inc=True)
            if "zq" in dbg and tau == 0:
                dmp["zq"] = p.dma("gpsimd", "dbg_zinv", dbg_dram("zinv", [16, 512]), zinv32[:, :], waits=[tk_zi])
            for ch in range(16):
                zbk = 7 if tau < NTAU - 1 else ch % 8
                scat_queue.append(make_scatter(tau, ch, oev_toks[ch], ohi_toks[ch], tk_zib, zbk))
        afm_partial = p.etok("vector")
        tk_pt_dead = p.etok("tensor")
        last_scat_tok = {}
        ich = 0
        while scat_queue:
            scat_queue.pop(0)()
            last_scat_tok[ich] = ost_tok[ich]
            ich += 1
        tk_afm_done = p.etok("vector")
        wb20_free = last_scat_tok.get(7, tk_afm_done)
        tk_b2T = p.dma("gpsimd", "sml2", b2T[:, :], b2_d,
                       waits=[tk_afm_done, p.etok("tensor")])
        tk_e80 = p.emit("gpsimd", lambda h: h.memset(e8[:, :], 1.0),
                        waits=[tk_afm_done, p.etok("tensor")], inc=True)
        tk_e8 = p.emit("gpsimd", lambda h: h.affine_select(
            out=e8[:, :], in_=e8[:, :], pattern=[[1, 8], [0, 128]],
            compare_op=ALU.is_equal, fill=0.0, base=0, channel_multiplier=-1),
            waits=[tk_e80], inc=True)
        if "afm" in dbg:
            dmp["afm"] = p.dma("gpsimd", "dbg_afm", dbg_dram("afm", [128, NCH * T], BF16), A_fm[:, :, :],
                               waits=[tk_afm_done])

        wo_slot_last = [None, None]
        res_tok = [None] * NTAU
        wo_epi = []

        def make_wo_epi2(c_o, tau, ev_tok):
            def emit_epi():
                bt = 4 + (c_o + tau) % 2
                tts = None
                for ss in range(4):
                    tts = p.emit("tensor", lambda h, c_o=c_o, tau=tau, ss=ss, bt=bt: h.transpose(
                        bankb(bt)[:, ss * 128:(ss + 1) * 128],
                        s_big[:, (c_o % 2) * NTAU * 512 + tau * 512 + ss * 128:
                              (c_o % 2) * NTAU * 512 + tau * 512 + (ss + 1) * 128],
                        ident[:, :]),
                        waits=[ev_tok, bank_tok[bt]], inc=(ss == 3))
                xv = x_tm[:, tau * 4096:(tau + 1) * 4096].rearrange(
                    "p (s r) -> p s r", r=1024)[:, :, c_o * 128:(c_o + 1) * 128]
                tk_r = p.emit("vector", lambda h, xv=xv, bt=bt: h.tensor_add(
                    xv, xv, bankb(bt)[:, 0:512].rearrange("p (s k) -> p s k", k=128)),
                    waits=[tts], inc=True)
                bank_tok[bt] = tk_r
                res_tok[tau] = tk_r
                sbig_war[c_o % 2] = p.etok("tensor")
            return emit_epi

        sbig_war = [None, None]
        tk_wo23 = p.dma("gpsimd", "wo2", woq23[:, 0:2048], wo_d[2],
                        waits=[p.etok("tensor"), p.etok("vector")])
        tk_wo23b = p.dma("gpsimd", "wo3", woq23[:, 2048:4096], wo_d[3],
                        waits=[p.etok("tensor"), p.etok("vector")])
        for tau0 in range(0, NTAU, 2):
            TP = min(2, NTAU - tau0)
            for c_o in range(NCH):
                qq = c_o // 2
                if qq < 2:
                    lsrc = wos[qq % 2]
                    lbase = 0
                    wtok = tk_wo[qq]
                else:
                    lsrc = woq23
                    lbase = (qq - 2) * 2048
                    wtok = tk_wo23 if qq == 2 else tk_wo23b
                mmt2 = [None] * TP
                for c_a in range(NCH):
                    for dt in range(TP):
                        tau = tau0 + dt
                        b = (c_o % 2) * 2 + dt
                        mmt2[dt] = p.emit("tensor", lambda h, c_o=c_o, tau=tau, c_a=c_a, b=b,
                                          lsrc=lsrc, lbase=lbase: h.matmul(
                            bank(b), lsrc[:, lbase + c_a * 256 + (c_o % 2) * 128:
                                          lbase + c_a * 256 + (c_o % 2) * 128 + 128],
                            A_fm[:, c_a, tau * 512:(tau + 1) * 512],
                            start=(c_a == 0), stop=(c_a == NCH - 1)),
                            waits=[afm_partial if c_a < 2 * (NTAU - 1) else tk_afm_done,
                                   wtok, bank_tok[b] if c_a == 0 else None],
                            inc=(c_a == NCH - 1))
                for dt in range(TP):
                    tau = tau0 + dt
                    b = (c_o % 2) * 2 + dt
                    tk_ev = p.emit("scalar", lambda h, c_o=c_o, tau=tau, b=b: h.activation(
                        s_big[:, (c_o % 2) * NTAU * 512 + tau * 512:
                              (c_o % 2) * NTAU * 512 + (tau + 1) * 512],
                        bank(b), AF.Identity, bias=bo_s[:, c_o:c_o + 1], scale=1.0),
                        waits=[mmt2[dt], tk_cst, sbig_war[c_o % 2]], inc=True)
                    bank_tok[b] = tk_ev
                    wo_epi.append(make_wo_epi2(c_o, tau, tk_ev))
                while len(wo_epi) > 1:
                    wo_epi.pop(0)()
        while wo_epi:
            wo_epi.pop(0)()
        tk_x1 = p.etok("vector")
        if "attn" in dbg:
            dmp["x1"] = p.dma("gpsimd", "dbg_x1", dbg_dram("x1", [128, NT * 1024]), x_tm[:, :], waits=[tk_x1])

        if dmp.get("q") is not None:
            p.emit("vector", lambda h: h.wait_ge(dmp["q"][0], dmp["q"][1]))
        xn2_touts = ln(xn2_fm, tk_x1, sq_war=p.etok("scalar"),
                       x_toks=[res_tok[i // max(NT // NTAU, 1)] for i in range(NT)])
        TPG2 = max(NT // NTAU, 1)
        xn2_tok = [xn2_touts[(g + 1) * TPG2 - 1] for g in range(NTAU)]
        tk_xn2 = xn2_touts[-1]

        tk_gt = None
        gslot_war = [None, None]
        gate_mm_toks = {}
        gate_cl = []

        def g_mm(tau):
            def f():
                b = tau % 2
                mms = None
                for c_e in range(NCH):
                    mms = p.emit("tensor", lambda h, tau=tau, c_e=c_e, b=b: h.matmul(
                        bank(b, lo=0, hi=8), gws[:, c_e * 8:(c_e + 1) * 8],
                        xn2_fm[:, c_e, tau * 512:(tau + 1) * 512],
                        start=(c_e == 0), stop=(c_e == NCH - 1)),
                        waits=[xn2_tok[tau], tk_sml, bank_tok[b]], inc=(c_e == NCH - 1))
                gate_mm_toks[tau] = mms
                bank_tok[b] = mms
            return f

        def g_tr(tau):
            def f():
                nonlocal tk_gt
                b = tau % 2
                tk_ge = None
                for s4 in range(4):
                    i = tau * 4 + s4
                    sl = i % 2
                    b2 = 2 + i % 2
                    tk_ge = p.emit("scalar", lambda h, s4=s4, sl=sl, b=b: h.activation(
                        G_sb[0:8, sl * 128:(sl + 1) * 128],
                        bank(b, lo=0, hi=8)[:, s4 * 128:(s4 + 1) * 128], AF.Identity,
                        bias=gb_s[0:8, 0:1], scale=1.0),
                        waits=[gate_mm_toks[tau], tk_cst, gslot_war[sl]], inc=True)
                    tts = p.emit("tensor", lambda h, sl=sl, b2=b2: h.transpose(
                        bank(b2, lo=0, hi=128, w=8), G_sb[0:8, sl * 128:(sl + 1) * 128],
                        identf[0:8, 0:8]),
                        waits=[tk_ge, tk_idf, bank_tok[b2]], inc=True)
                    gslot_war[sl] = tts
                    tk_gt = p.emit("vector", lambda h, i=i, b2=b2: h.tensor_copy(
                        Gtm[:, i * 8:(i + 1) * 8], bank(b2, w=8)),
                        waits=[tts, tk_afm_done], inc=True)
                    bank_tok[b2] = tk_gt
                bank_tok[b] = tk_ge
            return f

        def g_soft():
            nonlocal tk_gt
            GV = Gtm[:, :].rearrange("p (b j) -> p b j", j=8)
            PV_ = probs[:, :].rearrange("p (b j) -> p b j", j=8)
            WV = gtmp[:, :].rearrange("p (b j) -> p b j", j=8)
            def bc(t):
                return t[:, :].unsqueeze(2).broadcast_to([128, NT, 8])
            t1 = p.emit("vector", lambda h: h.reduce_max(gm1[:, :], GV, AX.X),
                        waits=[tk_gt], inc=True)
            t2 = p.emit("vector", lambda h: h.tensor_tensor(PV_, GV, bc(gm1), ALU.subtract),
                        waits=[t1], inc=True)
            t3 = p.emit("scalar", lambda h: h.activation(probs[:, :], probs[:, :], AF.Exp),
                        waits=[t2], inc=True)
            t4 = p.emit("vector", lambda h: h.reduce_sum(gm2[:, :], PV_, AX.X),
                        waits=[t3], inc=True)
            t5 = p.emit("vector", lambda h: h.reciprocal(gm1[:, :], gm2[:, :]),
                        waits=[t4], inc=True)
            t6 = p.emit("vector", lambda h: h.tensor_tensor(PV_, PV_, bc(gm1), ALU.mult),
                        waits=[t5], inc=True)
            t7 = p.emit("vector", lambda h: h.reduce_max(gm1[:, :], PV_, AX.X),
                        waits=[t6], inc=True)
            t8 = p.emit("vector", lambda h: h.tensor_tensor(WV, PV_, bc(gm1), ALU.is_ge),
                        waits=[t7], inc=True)
            t9 = p.emit("vector", lambda h: h.scalar_tensor_tensor(
                gtmp[:, :], gtmp[:, :], -1e30, probs[:, :], ALU.mult, ALU.add),
                waits=[t8], inc=True)
            t10 = p.emit("vector", lambda h: h.reduce_max(gm2[:, :], WV, AX.X),
                         waits=[t9], inc=True)
            t11 = p.emit("vector", lambda h: h.tensor_tensor(WV, PV_, bc(gm2), ALU.is_ge),
                         waits=[t10], inc=True)
            t12 = p.emit("vector", lambda h: h.tensor_tensor(
                Wtm[:, :], probs[:, :], gtmp[:, :], ALU.mult),
                waits=[t11], inc=True)
            return t12

        def g_wtmT(i0, tk_wtm):
            def f():
                nonlocal tk_wem
                for i in range(i0, min(i0 + 4, NT)):
                    b = i % 2
                    tts = p.emit("tensor", lambda h, i=i, b=b: h.transpose(
                        bankb(b)[0:8, 0:128], Wtm[:, i * 8:(i + 1) * 8], ident[:, :]),
                        waits=[tk_wtm, bank_tok[b]], inc=True)
                    tk_wem = p.emit("vector", lambda h, i=i, b=b: h.tensor_copy(
                        W_em[0:8, i * 128:(i + 1) * 128], bankb(b)[0:8, 0:128]),
                        waits=[tts, tk_afm_done], inc=True)
                    bank_tok[b] = tk_wem
            return f

        tk_wem = None

        def g_full(tau):
            def f():
                g_mm(tau)()
                g_tr(tau)()
            return f

        for tau in range(NTAU):
            gate_cl.append(g_full(tau))

        tk_seed = None
        def emit_web(e):
            tk_webl = None
            for tau in range(NTAU):
                b = tau % 2
                tk_wm = p.emit("tensor", lambda h, e=e, tau=tau, b=b: h.matmul(
                    bank(b), e8[0:8, e * 128:(e + 1) * 128],
                    W_em[0:8, tau * 512:(tau + 1) * 512],
                    start=True, stop=True),
                    waits=[tk_wem, tk_e8, bank_tok[b]], inc=True)
                tk_webl = p.emit("vector", lambda h, tau=tau, b=b: h.tensor_copy(
                    w_eb[:, tau * 512:(tau + 1) * 512], bank(b)),
                    waits=[tk_wm, tk_web_war, tk_afm_done], inc=True)
                bank_tok[b] = tk_webl
            return tk_webl

        tk_w1 = {}
        tk_w2 = {}
        w1_slot_last = [None, None]
        w2_slot_last = [None, None]

        def w1_stream(he, q, *wait_toks):
            buf = wb1b[q % 2] if he == 0 else wb1[q % 2]
            st = ("w1c" if q % 2 == 0 else "w1d") if he == 0 else ("w1a" if q % 2 == 0 else "w1b")
            tk_w1[(he, q)] = p.dma("gpsimd", st, buf[:, :], w1_d[he, q], waits=list(wait_toks))

        def w2_stream(he, q, *wait_toks):
            tk_w2[(he, q)] = p.dma("gpsimd", "w2a" if q % 2 == 0 else "w2b",
                                   wb2[q % 2][:, :], w2_d[he, q], waits=list(wait_toks))

        tk_free_r4 = p.etok("tensor")
        w1_stream(0, 0, tk_pt_dead, dmp.get("kv"), dmp.get("vaug"))
        w1_stream(0, 1, tk_pt_dead, dmp.get("kv"), dmp.get("vaug"))
        w2_stream(0, 0, wb20_free, p.etok("tensor"), dmp.get("kv"), dmp.get("vaug"))
        w2_stream(0, 1, tk_free_r4, tk_afm_done, dmp.get("kv"), dmp.get("vaug"))

        tk_hs_last = None
        tk_oacc = None
        tk_web_war = None
        oacc_co_tok = {}
        seed_toks = {}

        def emit_final_res(c_o, grp_free):
            for tau in range(NTAU):
                bt = grp_free + tau % 4
                tts = None
                for ss in range(4):
                    tts = p.emit("tensor", lambda h, c_o=c_o, tau=tau, ss=ss, bt=bt: h.transpose(
                        bankb(bt)[:, ss * 128:(ss + 1) * 128],
                        o_acc[:, c_o, tau * 512 + ss * 128: tau * 512 + (ss + 1) * 128],
                        ident[:, :]),
                        waits=[oacc_co_tok[c_o], bank_tok[bt]], inc=(ss == 3))
                xv = x_tm[:, tau * 4096:(tau + 1) * 4096].rearrange(
                    "p (s r) -> p s r", r=1024)[:, :, c_o * 128:(c_o + 1) * 128]
                tk_r2 = p.emit("vector", lambda h, xv=xv, bt=bt: h.tensor_add(
                    xv, xv, bankb(bt)[:, 0:512].rearrange("p (s k) -> p s k", k=128)),
                    waits=[tts], inc=True)
                bank_tok[bt] = tk_r2

        for he in range(16):
            e = he // 2
            half = he % 2
            hs_tok = {}
            for c_h in range(8):
                if he == 0:
                    if gate_cl:
                        gate_cl.pop(0)()
                    if not gate_cl and c_h == 4:
                        tk_wtm_f = g_soft()
                        for i0 in range(0, NT, 4):
                            gate_cl.append(g_wtmT(i0, tk_wtm_f))
                if c_h == 1 and half == 0 and he > 0:
                    tk_web = emit_web(e)
                q = c_h // 2
                grp = ((c_h + 1) % 2) * 4
                mmt = [None] * NTAU
                for c_f in range(NCH):
                    for tau in range(NTAU):
                        b = grp + tau
                        wsrc = wb1b[q % 2] if he == 0 else wb1[q % 2]
                        mmt[tau] = p.emit("tensor", lambda h, c_h=c_h, tau=tau, c_f=c_f, b=b, q=q, he=he, wsrc=wsrc: h.matmul(
                            bank(b), wsrc[:, c_f * 256 + (c_h % 2) * 128: c_f * 256 + (c_h % 2) * 128 + 128],
                            xn2_fm[:, c_f, tau * 512:(tau + 1) * 512],
                            start=(c_f == 0), stop=(c_f == NCH - 1)),
                            waits=[xn2_tok[tau], tk_w1[(he, q)], bank_tok[b] if c_f == 0 else None],
                            inc=(c_f == NCH - 1))
                bidx = e * 16 + half * 8 + c_h
                for tau in range(NTAU):
                    b = grp + tau
                    tk_hev = p.emit("scalar", lambda h, c_h=c_h, tau=tau, b=b, bidx=bidx: h.activation(
                        Hs[:, c_h, tau * 512:(tau + 1) * 512], bank(b), AF.Relu,
                        bias=b1_s[:, bidx:bidx + 1], scale=1.0),
                        waits=[mmt[tau], tk_cst, dmp.get("afm")], inc=True)
                    bank_tok[b] = tk_hev
                    tk_hs_last = tk_hev
                    hs_tok[c_h] = tk_hev
                    w1_slot_last[q % 2] = tk_hev
                if c_h % 2 == 1:
                    nq = q + 2
                    nhe, nqq = (he, nq) if nq < 4 else (he + 1, nq - 4)
                    if nhe < 16:
                        w1_stream(nhe, nqq, w1_slot_last[nq % 2])
            if he == 0:
                while gate_cl:
                    gate_cl.pop(0)()
                tk_web = emit_web(0)
            for c_o in range(NCH):
                q = c_o // 2
                grp = 4 if c_o % 2 == 0 else 0
                if he == 0:
                    sgrp = 0 if c_o % 2 == 0 else 4
                    seed_toks[c_o] = []
                    for tau in range(NTAU):
                        b = sgrp + tau
                        tk_sm = p.emit("tensor", lambda h, c_o=c_o, tau=tau, b=b: h.matmul(
                            bank(b), b2T[0:8, c_o * 128:(c_o + 1) * 128],
                            W_em[0:8, tau * 512:(tau + 1) * 512],
                            start=True, stop=True),
                            waits=[tk_wem, tk_b2T, bank_tok[b]], inc=True)
                        tk_seed = p.emit("vector", lambda h, c_o=c_o, tau=tau, b=b: h.tensor_copy(
                            o_acc[:, c_o, tau * 512:(tau + 1) * 512], bank(b)),
                            waits=[tk_sm, p.etok("scalar")], inc=True)
                        bank_tok[b] = tk_seed
                        seed_toks[c_o].append(tk_seed)
                mms = None
                for c_h in range(8):
                    for tau in range(NTAU):
                        b = grp + tau
                        mms = p.emit("tensor", lambda h, c_o=c_o, c_h=c_h, tau=tau, b=b, q=q, he=he: h.matmul(
                            bank(b), wb2[q % 2][:, c_h * 256 + (c_o % 2) * 128: c_h * 256 + (c_o % 2) * 128 + 128],
                            Hs[:, c_h, tau * 512:(tau + 1) * 512],
                            start=(c_h == 0), stop=(c_h == 7)),
                            waits=[hs_tok[c_h], tk_w2[(he, q)]] +
                                  ([bank_tok[b]] if c_h == 0 else []),
                            inc=(c_h == 7 and tau == NTAU - 1))
                tk_tmp = p.emit("vector", lambda h, grp=grp: h.tensor_tensor(
                    tmpb[:, :], pw[:, grp * 512:(grp + NTAU) * 512], w_eb[:, 0:NTAU * 512], ALU.mult),
                    waits=[mms, tk_web, tk_oacc], inc=True)
                tk_oacc = p.emit("vector", lambda h, c_o=c_o: h.tensor_add(
                    o_acc[:, c_o, :], o_acc[:, c_o, :], tmpb[:, 0:T]),
                    waits=[tk_tmp, seed_toks[c_o][-1]], inc=True)
                tk_web_war = tk_tmp
                for tau in range(NTAU):
                    bank_tok[grp + tau] = tk_oacc
                w2_slot_last[q % 2] = tk_oacc
                if he == 15:
                    oacc_co_tok[c_o] = tk_oacc
                    if c_o >= 1:
                        emit_final_res(c_o - 1, 4 - grp if grp in (0, 4) else 0)
                if c_o % 2 == 1:
                    nq = q + 2
                    nhe, nqq = (he, nq) if nq < 4 else (he + 1, nq - 4)
                    if nhe < 16:
                        w2_stream(nhe, nqq, w2_slot_last[nq % 2])
        if "ffnacc" in dbg:
            p.dma("gpsimd", "dbg_oacc", dbg_dram("oacc", [128, NCH * T], BF16), o_acc[:, :, :],
                  waits=[tk_oacc])

        emit_final_res(NCH - 1, 0)
        p.dma("sync", "out", y_d, x_tm[:, :], waits=[p.etok("vector")])
        for snm in list(p.dma_sem.keys()):
            if p.dma_cnt[snm]:
                p.emit("sync", lambda h, snm=snm: h.wait_ge(p.dma_sem[snm], p.dma_cnt[snm]))

        p.finalize()
    return nc, dbg_out


def prep_inputs(ins, T=2048):
    NT = T // 128
    o = {}
    x = np.ascontiguousarray(np.asarray(ins["x"]), dtype=np.float32)
    o["x"] = np.ascontiguousarray(x.reshape(NT, 128, 1024).transpose(1, 0, 2).reshape(128, NT * 1024))
    wq = np.ascontiguousarray(np.asarray(ins["Wq"]).T)
    o["wq"] = np.ascontiguousarray(
        wq.reshape(8, 128, 4, 256).transpose(2, 1, 0, 3).reshape(4, 128, 8 * 256))
    wo = np.ascontiguousarray(np.asarray(ins["Wo"]).T)
    o["wo"] = np.ascontiguousarray(
        wo.reshape(8, 128, 4, 256).transpose(2, 1, 0, 3).reshape(4, 128, 8 * 256))
    wkv = np.ascontiguousarray(np.asarray(ins["Wkv"]).T)
    o["wkv"] = np.ascontiguousarray(wkv.reshape(8, 128, 128).transpose(1, 0, 2).reshape(128, 8 * 128))
    o["bq"] = np.ascontiguousarray(np.asarray(ins["bq"]).reshape(8, 128).T)
    o["bo"] = np.ascontiguousarray(np.asarray(ins["bo"]).reshape(8, 128).T)
    o["bkv"] = np.ascontiguousarray(np.asarray(ins["bkv"]).reshape(128, 1))
    gw = np.ascontiguousarray(np.asarray(ins["gate_w"]).T)
    o["gw"] = np.ascontiguousarray(gw.reshape(8, 128, 8).transpose(1, 0, 2).reshape(128, 64))
    o["gb"] = np.ascontiguousarray(np.asarray(ins["gate_b"]).reshape(8, 1))
    b1 = np.asarray(ins["b1"])
    o["b1"] = np.ascontiguousarray(b1.reshape(8, 16, 128).transpose(2, 0, 1).reshape(128, 128))
    o["b2"] = np.ascontiguousarray(np.asarray(ins["b2"]))
    w1 = np.asarray(ins["w1"])
    w1p = np.empty((16, 4, 128, 8 * 256), np.float32)
    for e in range(8):
        for half in range(2):
            for q in range(4):
                blk = w1[e][:, half * 1024 + q * 256: half * 1024 + (q + 1) * 256]
                w1p[e * 2 + half, q] = blk.reshape(8, 128, 256).transpose(1, 0, 2).reshape(128, 8 * 256)
    o["w1"] = w1p
    w2 = np.asarray(ins["w2"])
    w2p = np.empty((16, 4, 128, 8 * 256), np.float32)
    for e in range(8):
        for half in range(2):
            for q in range(4):
                blk = w2[e][half * 1024:(half + 1) * 1024, q * 256:(q + 1) * 256]
                w2p[e * 2 + half, q] = blk.reshape(8, 128, 256).transpose(1, 0, 2).reshape(128, 8 * 256)
    o["w2"] = w2p
    return o


def unprep_output(y, T=2048):
    NT = T // 128
    return np.ascontiguousarray(np.asarray(y).reshape(128, NT, 1024).transpose(1, 0, 2).reshape(T, 1024))


def prep_x(x, T=2048):
    NT = T // 128
    x = np.ascontiguousarray(np.asarray(x), dtype=np.float32)
    return np.ascontiguousarray(x.reshape(NT, 128, 1024).transpose(1, 0, 2).reshape(128, NT * 1024))


def prep_inputs(ins, T=2048):
    NT = T // 128
    o = {}
    x = np.ascontiguousarray(np.asarray(ins["x"]), dtype=np.float32)
    o["x"] = np.ascontiguousarray(x.reshape(NT, 128, 1024).transpose(1, 0, 2).reshape(128, NT * 1024))
    wq = np.ascontiguousarray(np.asarray(ins["Wq"]).T)
    o["wq"] = np.ascontiguousarray(
        wq.reshape(8, 128, 4, 256).transpose(2, 1, 0, 3).reshape(4, 128, 8 * 256))
    wo = np.ascontiguousarray(np.asarray(ins["Wo"]).T)
    o["wo"] = np.ascontiguousarray(
        wo.reshape(8, 128, 4, 256).transpose(2, 1, 0, 3).reshape(4, 128, 8 * 256))
    wkv = np.ascontiguousarray(np.asarray(ins["Wkv"]).T)
    o["wkv"] = np.ascontiguousarray(wkv.reshape(8, 128, 128).transpose(1, 0, 2).reshape(128, 8 * 128))
    o["bq"] = np.ascontiguousarray(np.asarray(ins["bq"]).reshape(8, 128).T)
    o["bo"] = np.ascontiguousarray(np.asarray(ins["bo"]).reshape(8, 128).T)
    o["bkv"] = np.ascontiguousarray(np.asarray(ins["bkv"]).reshape(128, 1))
    gw = np.ascontiguousarray(np.asarray(ins["gate_w"]).T)
    o["gw"] = np.ascontiguousarray(gw.reshape(8, 128, 8).transpose(1, 0, 2).reshape(128, 64))
    o["gb"] = np.ascontiguousarray(np.asarray(ins["gate_b"]).reshape(8, 1))
    b1 = np.asarray(ins["b1"])
    o["b1"] = np.ascontiguousarray(b1.reshape(8, 16, 128).transpose(2, 0, 1).reshape(128, 128))
    o["b2"] = np.ascontiguousarray(np.asarray(ins["b2"]))
    w1 = np.asarray(ins["w1"])
    w1p = np.empty((16, 4, 128, 8 * 256), np.float32)
    for e in range(8):
        for half in range(2):
            for q in range(4):
                blk = w1[e][:, half * 1024 + q * 256: half * 1024 + (q + 1) * 256]
                w1p[e * 2 + half, q] = blk.reshape(8, 128, 256).transpose(1, 0, 2).reshape(128, 8 * 256)
    o["w1"] = w1p
    w2 = np.asarray(ins["w2"])
    w2p = np.empty((16, 4, 128, 8 * 256), np.float32)
    for e in range(8):
        for half in range(2):
            for q in range(4):
                blk = w2[e][half * 1024:(half + 1) * 1024, q * 256:(q + 1) * 256]
                w2p[e * 2 + half, q] = blk.reshape(8, 128, 256).transpose(1, 0, 2).reshape(128, 8 * 256)
    o["w2"] = w2p
    return o


def unprep_output(y, T=2048):
    NT = T // 128
    return np.ascontiguousarray(np.asarray(y).reshape(128, NT, 1024).transpose(1, 0, 2).reshape(T, 1024))

from concourse.bass_utils import run_bass_kernel_spmd

_T = 2048


def kernel(L0_attn_x, L0_attn_Wq, L0_attn_bq, L0_attn_Wkv, L0_attn_bkv,
           L0_attn_Wo, L0_attn_bo, L0_ffn_gate_w, L0_ffn_gate_b,
           L0_ffn_w1, L0_ffn_b1, L0_ffn_w2, L0_ffn_b2):
    x = np.ascontiguousarray(np.asarray(L0_attn_x, dtype=np.float32))
    B = x.shape[0]
    shared = {
        "Wq": L0_attn_Wq, "bq": L0_attn_bq, "Wkv": L0_attn_Wkv, "bkv": L0_attn_bkv,
        "Wo": L0_attn_Wo, "bo": L0_attn_bo, "gate_w": L0_ffn_gate_w,
        "gate_b": L0_ffn_gate_b, "w1": L0_ffn_w1, "b1": L0_ffn_b1,
        "w2": L0_ffn_w2, "b2": L0_ffn_b2,
    }
    base = prep_inputs(dict(shared, x=x[0]), T=_T)
    in_maps = []
    for b in range(B):
        m = dict(base)
        m["x"] = prep_x(x[b], T=_T)
        in_maps.append(m)
    nc, _ = build(T=_T)
    res = run_bass_kernel_spmd(nc, in_maps, core_ids=list(range(B)))
    out = np.empty((B, _T, 1024), np.float32)
    for b in range(B):
        out[b] = unprep_output(np.asarray(res.results[b]["y"]), T=_T)
    return out


# version 22
# speedup vs baseline: 1.0345x; 1.0003x over previous
import contextlib
import numpy as np
import concourse.bass as bass
import concourse.mybir as mybir

F32 = mybir.dt.float32
BF16 = mybir.dt.bfloat16
AF = mybir.ActivationFunctionType
ALU = mybir.AluOpType
AX = mybir.AxisListType

ENGINES = ("tensor", "vector", "scalar", "gpsimd", "sync")


class Prog:
    def __init__(self, nc, stack, dma_streams=()):
        self.nc = nc
        self.stack = stack
        self.ops = {e: [] for e in ENGINES}
        self.sem = {e: stack.enter_context(nc.semaphore(f"sem_{e}")) for e in ENGINES}
        self.cnt = {e: 0 for e in ENGINES}
        self.dma_sem = {}
        self.dma_cnt = {}
        for name in dma_streams:
            self.dma_sem[name] = stack.enter_context(nc.semaphore(f"d_{name}"))
            self.dma_cnt[name] = 0
        self._waited = {e: {} for e in ENGINES}

    def _wait(self, eng, tok):
        if tok is None:
            return
        sem, val = tok
        key = sem.name if hasattr(sem, "name") else id(sem)
        if self._waited[eng].get(key, -1) >= val:
            return
        self._waited[eng][key] = val
        self.ops[eng].append(lambda h, sem=sem, val=val: h.wait_ge(sem, val))

    def emit(self, eng, fn, waits=(), inc=False):
        for t in waits:
            self._wait(eng, t)
        if inc:
            self.cnt[eng] += 1
            v = self.cnt[eng]
            sem = self.sem[eng]

            def run(h, fn=fn, sem=sem):
                fn(h).then_inc(sem, 1)

            self.ops[eng].append(run)
            return (sem, v)
        self.ops[eng].append(lambda h, fn=fn: fn(h))
        return None

    def dma(self, q, stream, out, in_, waits=(), **kw):
        if stream not in self.dma_sem:
            self.dma_sem[stream] = self.stack.enter_context(self.nc.semaphore(f"d_{stream}"))
            self.dma_cnt[stream] = 0
        for t in waits:
            self._wait(q, t)
        self.dma_cnt[stream] += 16
        v = self.dma_cnt[stream]
        sem = self.dma_sem[stream]
        self.ops[q].append(
            lambda h, out=out, in_=in_, sem=sem: h.dma_start(out=out, in_=in_, **kw).then_inc(sem, 16)
        )
        return (sem, v)

    def etok(self, eng):
        return (self.sem[eng], self.cnt[eng])

    def finalize(self):
        with self.nc.Block() as block:
            for name, deco in (("sync", block.sync), ("gpsimd", block.gpsimd),
                               ("scalar", block.scalar), ("vector", block.vector),
                               ("tensor", block.tensor)):
                if self.ops[name]:
                    ops = self.ops[name]

                    def runner(h, ops=ops):
                        for f in ops:
                            f(h)
                    deco(runner)


def build(T=2048, dbg=()):
    NT = T // 128
    SC = NT
    NTAU = T // 512
    HB = T // 16
    UB = 512 // HB
    NCH = 8
    dbg = set(dbg)
    assert NTAU <= 4

    nc = bass.Bass("TRN2", target_bir_lowering=False)

    def dram_in(name, shape):
        return nc.dram_tensor(name, shape, F32, kind="ExternalInput").ap()

    x_d = dram_in("x", [128, NT * 1024])
    wq_d = dram_in("wq", [4, 128, 8 * 256])
    wo_d = dram_in("wo", [4, 128, 8 * 256])
    wkv_d = dram_in("wkv", [128, 8 * 128])
    bq_d = dram_in("bq", [128, 8])
    bo_d = dram_in("bo", [128, 8])
    bkv_d = dram_in("bkv", [128, 1])
    gw_d = dram_in("gw", [128, 64])
    gb_d = dram_in("gb", [8, 1])
    b1_d = dram_in("b1", [128, 128])
    b2_d = dram_in("b2", [8, 1024])
    w1_d = dram_in("w1", [16, 4, 128, 8 * 256])
    w2_d = dram_in("w2", [16, 4, 128, 8 * 256])
    y_d = nc.dram_tensor("y", [128, NT * 1024], F32, kind="ExternalOutput").ap()

    dbg_out = {}
    def dbg_dram(name, shape, dtype=F32):
        dbg_out[name] = nc.dram_tensor("dbg_" + name, shape, dtype, kind="ExternalOutput").ap()
        return dbg_out[name]

    with contextlib.ExitStack() as stack:
        OFF = [16768]
        def sb(name, shape, dtype=F32, at=None):
            nbytes = int(np.prod(shape[1:])) * mybir.dt.size(dtype)
            if at is None:
                at = OFF[0]
            OFF[0] = max(OFF[0], at + ((nbytes + 31) // 32) * 32)
            return nc.alloc_sbuf_tensor_at(name, list(shape), dtype, offset=at)

        x_tm = sb("x_tm", [128, NT * 1024], F32)
        r1 = OFF[0]
        xn1_fm = sb("xn1_fm", [128, NCH, T], BF16)
        A_fm = sb("A_fm", [128, NCH, T], BF16, at=r1)
        Hs = sb("Hs", [128, NCH, T], BF16, at=r1)
        r2 = OFF[0]
        q_fm = sb("q_fm", [128, NCH, T], BF16)
        xn2_fm = sb("xn2_fm", [128, NCH, T], BF16, at=r2)
        r3 = OFF[0]
        PT_A = sb("PT_A", [128, SC * 512], BF16)
        PT_B = sb("PT_B", [128, SC * 512], BF16)
        o_acc = sb("o_acc", [128, NCH, T], BF16, at=r3)
        qsq = sb("qsq", [128, 512], BF16, at=r3)
        qsq2 = sb("qsq2", [128, 512], BF16, at=r3 + 1024)
        sqd = sb("sqd", [128, 1024], BF16, at=r3 + 2048)
        s_big = sb("s_big", [128, 2 * NTAU * 512], BF16, at=r3 + 4096)
        woq23 = sb("woq23", [128, 2 * 2048], BF16, at=r3 + 12288)
        wb1b = [sb("wb1b0", [128, 8 * 256], BF16, at=r3 + 20480),
                sb("wb1b1", [128, 8 * 256], BF16, at=r3 + 24576)]
        r4 = OFF[0]
        wqs = [sb("wqs0", [128, 8 * 256], BF16), sb("wqs1", [128, 8 * 256], BF16)]
        wos = [sb("wos0", [128, 8 * 256], BF16, at=r4),
               sb("wos1", [128, 8 * 256], BF16, at=r4 + 4096)]
        wb1 = [sb("wb1_0", [128, 8 * 256], BF16, at=r4),
               sb("wb1_1", [128, 8 * 256], BF16, at=r4 + 4096)]
        r5 = OFF[0]
        kv_fm = sb("kv_fm", [128, T], BF16)
        X0 = OFF[0]
        khi = sb("khi", [128, T], BF16, at=X0)
        O_st = sb("O_st", [64, 16 * 512], BF16, at=X0)
        wb2 = [sb("wb2_0", [128, 8 * 256], BF16, at=X0),
               sb("wb2_1", [128, 8 * 256], BF16, at=X0 + 4096)]
        O_hi = sb("O_hi", [128, 16 * 256], BF16, at=X0 + 4096)
        w_eb = sb("w_eb", [128, T], BF16, at=X0 + 8192)
        tmpb = sb("tmpb", [128, T], BF16, at=X0 + 8192 + 2 * T)
        OFF[0] = max(OFF[0], X0 + 16384, X0 + 8192 + 4 * T)
        v_aug = sb("v_aug", [128, SC * 65], BF16)
        wkvs = sb("wkvs", [128, 8 * 128], BF16, at=OFF[0] - ((SC * 65 * 2 + 31) // 32) * 32)
        xns_base = OFF[0]
        xns = [sb("xns0", [128, 1024], BF16), sb("xns1", [128, 1024], BF16)]
        W_em = sb("W_em", [8, T], BF16, at=xns_base)
        rs = OFF[0]
        zr_st = sb("zr_st", [65, 512], BF16)
        zinv32 = sb("zinv32", [16, 512], F32)
        e16t = sb("e16t", [16, 16 * 128], BF16)
        attn_end = OFF[0]
        OFF[0] = rs
        Gtm = sb("Gtm", [128, NT * 8], F32)
        probs = sb("probs", [128, NT * 8], F32)
        Wtm = sb("Wtm", [128, NT * 8], BF16)
        b2T = sb("b2T", [8, 1024], BF16)
        e8 = sb("e8", [8, 8 * 128], BF16)
        OFF[0] = max(OFF[0], attn_end)
        gm1 = sb("gm1", [128, NT], F32)
        gm1n = sb("gm1n", [128, NT], F32)
        gm2 = sb("gm2", [128, NT], F32)
        gtmp = sb("gtmp", [128, NT * 8], F32)
        G_sb = sb("G_sb", [8, 256], F32)
        ident = sb("ident", [128, 128], BF16)
        identf = sb("identf", [8, 8], F32)
        onesb = sb("onesb", [128, 128], BF16)
        zsel16 = sb("zsel16", [65, 256], BF16)
        qsel = sb("qsel", [128, 16], BF16)
        stt4 = sb("stt4", [4, 1], F32)
        gws = sb("gws", [128, 64], BF16)
        bq_s = sb("bq_s", [128, 8], F32)
        bo_s = sb("bo_s", [128, 8], F32)
        bkv_s = sb("bkv_s", [128, 1], F32)
        gb_s = sb("gb_s", [8, 1], F32)
        b1_s = sb("b1_s", [128, 128], F32)
        sums = sb("sums", [128, NT], F32)
        sqs = sb("sqs", [128, NT], F32)
        mean = sb("mean", [128, NT], F32)
        inv = sb("inv", [128, NT], F32)

        stt1 = sb("stt1", [1, 8], F32)
        negb = sb("negb", [1, 1], BF16)
        negB_b = sb("negB_b", [128, 1], F32)

        used = OFF[0]
        assert used <= 229376, f"SBUF over budget: {used}"

        pw = stack.enter_context(nc.psum_tensor("pw", [128, 8 * 512], F32))
        pwb = pw.bitcast(BF16)

        def bank(i, lo=0, hi=128, w=512):
            return pw[lo:hi, i * 512:i * 512 + w]

        def bankb(i, lo=0, hi=128, w=1024):
            return pwb[lo:hi, i * 1024:i * 1024 + w]

        streams = ["cst", "sml", "wqa", "wqb", "woa", "wob",
                   "w1a", "w1b", "w2a", "w2b", "kvcp", "out"]
        p = Prog(nc, stack, dma_streams=streams)
        dmp = {}
        qstat_mm = {}
        qstat_bank = {}
        sq_slot_tok = [None, None]
        sq_slot_wr = [None, None]
        bank_tok = [None] * 8

        CH4 = NT * 1024 // 4
        tk_xc = [p.dma("sync", f"x{i}", x_tm[:, i * CH4:(i + 1) * CH4],
                       x_d[:, i * CH4:(i + 1) * CH4]) for i in range(4)]
        for dst, src in ((bq_s, bq_d), (bo_s, bo_d), (bkv_s, bkv_d), (gb_s, gb_d), (b1_s, b1_d)):
            p.dma("sync", "cst", dst[:, :], src)
        tk_cst = (p.dma_sem["cst"], p.dma_cnt["cst"])
        p.dma("gpsimd", "sml", wkvs[:, :], wkv_d)
        p.dma("gpsimd", "sml", gws[:, :], gw_d)
        tk_sml = (p.dma_sem["sml"], p.dma_cnt["sml"])

        tk_ones = p.emit("gpsimd", lambda h: h.memset(onesb[:, :], 1.0), inc=True)
        tk_id = p.emit("gpsimd", lambda h: h.affine_select(
            out=ident[:, :], in_=onesb[:, :], pattern=[[1, 128]],
            compare_op=ALU.is_equal, fill=0.0, base=0, channel_multiplier=-1),
            waits=[tk_ones], inc=True)
        tk_if0 = p.emit("gpsimd", lambda h: h.memset(identf[:, :], 1.0), inc=True)
        tk_idf = p.emit("gpsimd", lambda h: h.affine_select(
            out=identf[:, :], in_=identf[:, :], pattern=[[1, 8]],
            compare_op=ALU.is_equal, fill=0.0, base=0, channel_multiplier=-1),
            waits=[tk_if0], inc=True)
        tk_qs0 = p.emit("gpsimd", lambda h: h.memset(qsel[:, :], 1.0), inc=True)
        tk_qsel = p.emit("gpsimd", lambda h: h.affine_select(
            out=qsel[:, :], in_=qsel[:, :], pattern=[[1, 4], [-1, 4]],
            compare_op=ALU.is_equal, fill=0.0, base=0, channel_multiplier=0),
            waits=[tk_qs0], inc=True)
        tk_z0 = p.emit("gpsimd", lambda h: h.memset(zsel16[:, :], 1.0), inc=True)
        tk_zsel = p.emit("gpsimd", lambda h: h.affine_select(
            out=zsel16[:, :], in_=zsel16[:, :], pattern=[[1, 16], [-1, 16]],
            compare_op=ALU.is_equal, fill=0.0, base=0, channel_multiplier=0),
            waits=[tk_z0], inc=True)
        tk_e0 = p.emit("gpsimd", lambda h: h.memset(e16t[:, :], 1.0), inc=True)
        tk_e16 = p.emit("gpsimd", lambda h: h.affine_select(
            out=e16t[:, :], in_=e16t[:, :], pattern=[[1, 16], [0, 128]],
            compare_op=ALU.is_equal, fill=0.0, base=0, channel_multiplier=-1),
            waits=[tk_e0], inc=True)
        tk_bq = p.emit("vector", lambda h: h.tensor_scalar_mul(bq_s[:, :], bq_s[:, :], 0.125),
                       waits=[tk_cst], inc=True)

        tk_wq = [None] * 4
        tk_wq[0] = p.dma("gpsimd", "wqa", wqs[0][:, :], wq_d[0])
        tk_wq[1] = p.dma("gpsimd", "wqb", wqs[1][:, :], wq_d[1])

        def ln(xn_dst, x_tok, sq_war=None, x_toks=None):
            TPG = max(NT // NTAU, 1)
            xt = (lambda i: x_toks[min(i * 4 // NT, len(x_toks) - 1)] if len(x_toks) == 4
                  else x_toks[i]) if x_toks else (lambda i: x_tok)
            sum_tok = [None] * NTAU
            sq_tok = [None] * NTAU
            inv_tok = [None] * NTAU
            tk_sq = sq_war
            for g in range(NTAU):
                for i in range(g * TPG, (g + 1) * TPG):
                    sum_tok[g] = p.emit("vector", lambda h, i=i: h.reduce_sum(
                        sums[:, i:i + 1], x_tm[:, i * 1024:(i + 1) * 1024], AX.X),
                        waits=[xt(i)], inc=(i == (g + 1) * TPG - 1))
                for i in range(g * TPG, (g + 1) * TPG):
                    tk_sq = p.emit("scalar", lambda h, i=i: h.activation(
                        sqd[:, :], x_tm[:, i * 1024:(i + 1) * 1024], AF.Square,
                        accum_out=sqs[:, i:i + 1]),
                        waits=[xt(i), tk_sq], inc=True)
                sq_tok[g] = tk_sq
                sl = slice(g * TPG, (g + 1) * TPG)
                tk_mean = p.emit("vector", lambda h, sl=sl: h.tensor_scalar_mul(
                    mean[:, sl], sums[:, sl], 1.0 / 1024),
                    waits=[sum_tok[g]], inc=True)
                tk_m2 = p.emit("vector", lambda h, sl=sl: h.tensor_mul(
                    inv[:, sl], mean[:, sl], mean[:, sl]), waits=[tk_mean], inc=True)
                tk_var = p.emit("vector", lambda h, sl=sl: h.scalar_tensor_tensor(
                    inv[:, sl], sqs[:, sl], 1.0 / 1024, inv[:, sl], ALU.mult, ALU.subtract),
                    waits=[tk_m2, sq_tok[g]], inc=True)
                tk_eps = p.emit("vector", lambda h, sl=sl: h.tensor_scalar_add(
                    inv[:, sl], inv[:, sl], 1e-5), waits=[tk_var], inc=True)
                tk_std = p.emit("scalar", lambda h, sl=sl: h.activation(
                    inv[:, sl], inv[:, sl], AF.Sqrt), waits=[tk_eps], inc=True)
                inv_tok[g] = p.emit("vector", lambda h, sl=sl: h.reciprocal(
                    inv[:, sl], inv[:, sl]), waits=[tk_std], inc=True)
            touts = [None] * NT
            for i in range(NT):
                g = i // TPG
                slot = xns[i % 2]
                b = i % 2
                tk_xn = p.emit("vector", lambda h, i=i, slot=slot: h.tensor_scalar(
                    slot[:, :], x_tm[:, i * 1024:(i + 1) * 1024],
                    mean[:, i:i + 1], inv[:, i:i + 1], ALU.subtract, ALU.mult),
                    waits=[inv_tok[g], bank_tok[i % 2], xt(i)], inc=True)
                tts = None
                for c in range(NCH):
                    tts = p.emit("tensor", lambda h, c=c, b=b, slot=slot: h.transpose(
                        bankb(b)[:, c * 128:(c + 1) * 128],
                        slot[:, c * 128:(c + 1) * 128], ident[:, :]),
                        waits=[tk_xn, tk_id, bank_tok[b]], inc=(c == NCH - 1))
                touts[i] = p.emit("vector", lambda h, i=i, b=b: h.tensor_copy(
                    xn_dst[:, :, i * 128:(i + 1) * 128],
                    bankb(b)[:, :].rearrange("p (c k) -> p c k", k=128)),
                    waits=[tts], inc=True)
                bank_tok[b] = touts[i]
            return touts

        tk_xn1 = ln(xn1_fm, tk_xc[-1], x_toks=tk_xc)[-1]
        if "xn1" in dbg:
            dmp["r1"] = p.dma("gpsimd", "dbg_xn1", dbg_dram("xn1", [128, NCH * T], BF16),
                              xn1_fm[:, :, :], waits=[tk_xn1])

        tk_q_evac = None
        wq_slot_last = [None, None]
        qstat_bank_all = bank(5, lo=0, hi=4)
        for c_o in range(NCH):
            qq = c_o // 2
            mmt = [None] * NTAU
            for c_e in range(NCH):
                for tau in range(NTAU):
                    b = tau % 4
                    mmt[tau] = p.emit("tensor", lambda h, c_o=c_o, tau=tau, c_e=c_e, b=b, qq=qq: h.matmul(
                        bank(b), wqs[qq % 2][:, c_e * 256 + (c_o % 2) * 128: c_e * 256 + (c_o % 2) * 128 + 128],
                        xn1_fm[:, c_e, tau * 512:(tau + 1) * 512],
                        start=(c_e == 0), stop=(c_e == NCH - 1)),
                        waits=[tk_xn1, tk_wq[qq], bank_tok[b] if c_e == 0 else None],
                        inc=(c_e == NCH - 1))
            for tau in range(NTAU):
                b = tau % 4
                tk_q_evac = p.emit("scalar", lambda h, c_o=c_o, tau=tau, b=b: h.activation(
                    q_fm[:, c_o, tau * 512:(tau + 1) * 512], bank(b), AF.Identity,
                    bias=bq_s[:, c_o:c_o + 1], scale=0.125),
                    waits=[mmt[tau], tk_bq], inc=True)
                bank_tok[b] = tk_q_evac
                wq_slot_last[qq % 2] = tk_q_evac
                slot = qsq if (c_o % 2 == 0) else qsq2
                tk_sqc = p.emit("scalar", lambda h, c_o=c_o, tau=tau, slot=slot: h.activation(
                    slot[:, :], q_fm[:, c_o, tau * 512:(tau + 1) * 512], AF.Square),
                    waits=[tk_q_evac, sq_slot_tok[c_o % 2], sq_slot_wr[c_o % 2]], inc=True)
                sq_slot_wr[c_o % 2] = tk_sqc
                qstat_mm[tau] = p.emit("tensor", lambda h, c_o=c_o, tau=tau, slot=slot: h.matmul(
                    qstat_bank_all, qsel[:, (tau % 4) * 4:(tau % 4) * 4 + 4], slot[:, :],
                    start=(c_o == 0 and tau == 0), stop=(c_o == NCH - 1 and tau == NTAU - 1)),
                    waits=[tk_sqc, tk_qsel] + ([bank_tok[5]] if c_o == 0 and tau == 0 else []),
                    inc=True)
                sq_slot_tok[c_o % 2] = qstat_mm[tau]
            if c_o % 2 == 1:
                nq = qq + 2
                if nq < 4:
                    tk_wq[nq] = p.dma("gpsimd", "wqa" if nq % 2 == 0 else "wqb",
                                      wqs[nq % 2][:, :], wq_d[nq], waits=[wq_slot_last[nq % 2]])

        tk_wo = [None] * 4
        tk_wo[0] = p.dma("gpsimd", "woa", wos[0][:, :], wo_d[0], waits=[p.etok("tensor")])
        tk_wo[1] = p.dma("gpsimd", "wob", wos[1][:, :], wo_d[1])

        tk_kv_evac = None
        mmt = [None] * NTAU
        for c_e in range(NCH):
            for tau in range(NTAU):
                b = tau % 4
                mmt[tau] = p.emit("tensor", lambda h, tau=tau, c_e=c_e, b=b: h.matmul(
                    bank(b), wkvs[:, c_e * 128:(c_e + 1) * 128],
                    xn1_fm[:, c_e, tau * 512:(tau + 1) * 512],
                    start=(c_e == 0), stop=(c_e == NCH - 1)),
                    waits=[tk_xn1, tk_sml, bank_tok[b] if c_e == 0 else None],
                    inc=(c_e == NCH - 1))
        for tau in range(NTAU):
            b = tau % 4
            tk_kv_evac = p.emit("scalar", lambda h, tau=tau, b=b: h.activation(
                kv_fm[:, tau * 512:(tau + 1) * 512], bank(b), AF.Identity,
                bias=bkv_s[:, 0:1], scale=1.0),
                waits=[mmt[tau], tk_cst], inc=True)
            bank_tok[b] = tk_kv_evac

        tk_vones = p.emit("gpsimd", lambda h: h.memset(
            v_aug[:, :].rearrange("p (j q) -> p j q", q=65)[:, :, 64:65], 1.0),
            waits=[p.etok("tensor")], inc=True)
        tk_kvx2 = p.dma("sync", "kvcp", khi[64:128, :], kv_fm[0:64, :], waits=[tk_kv_evac])
        tk_kvx = (p.dma_sem["kvcp"], p.dma_cnt["kvcp"])

        vts = None
        for j in range(SC):
            vts = p.emit("tensor", lambda h, j=j: h.transpose(
                bankb(4)[:, j * 64:(j + 1) * 64],
                kv_fm[64:128, j * 128:(j + 1) * 128], ident[64:128, 64:128]),
                waits=[tk_kv_evac, tk_id, bank_tok[4]], inc=(j == SC - 1))
        tk_vaug = p.emit("vector", lambda h: h.tensor_copy(
            v_aug[:, :].rearrange("p (j q) -> p j q", q=65)[:, :, 0:64],
            bankb(4, w=SC * 64)[:, :].rearrange("p (j q) -> p j q", q=64)),
            waits=[vts, tk_vones], inc=True)
        bank_tok[4] = tk_vaug

        tk_qmax = tk_kmax = None
        for which in range(2):
            src_is_q = (which == 0)
            if src_is_q:
                tk_r = p.emit("vector", lambda h: h.reduce_max(
                    stt4[0:4, 0:1], bank(5, lo=0, hi=4), AX.X),
                    waits=[qstat_mm[NTAU - 1]], inc=True)
                bank_tok[5] = tk_r
                tk_tr = p.emit("tensor", lambda h: h.transpose(
                    bank(6, lo=0, hi=1, w=4), stt4[0:4, 0:1], identf[0:4, 0:4]),
                    waits=[tk_r, tk_idf, bank_tok[6]], inc=True)
                tk_qmax = p.emit("vector", lambda h: h.reduce_max(
                    stt1[0:1, 0:1], bank(6, lo=0, hi=1, w=4), AX.X),
                    waits=[tk_tr], inc=True)
                bank_tok[6] = tk_qmax
                continue
            nchunk = 1
            first = True
            tk_r = None
            for tau in range(NTAU):
                mms = None
                for c in range(nchunk):
                    slot = qsq if (c % 2 == 0) else qsq2
                    tk_sqc = p.emit("scalar", lambda h, c=c, tau=tau, slot=slot: h.activation(
                        slot[:, :], kv_fm[:, tau * 512:(tau + 1) * 512], AF.Square),
                        waits=[tk_kv_evac, sq_slot_tok[c % 2], sq_slot_wr[c % 2]], inc=True)
                    sq_slot_wr[c % 2] = tk_sqc
                    mms = p.emit("tensor", lambda h, c=c, nchunk=nchunk, slot=slot: h.matmul(
                        bank(5, lo=0, hi=1), onesb[:, 0:1], slot[:, :],
                        start=(c == 0), stop=(c == nchunk - 1)),
                        waits=[tk_sqc, tk_ones, bank_tok[5]], inc=True)
                    sq_slot_tok[c % 2] = mms
                dst = which if first else which + 2
                tk_r = p.emit("vector", lambda h, dst=dst: h.reduce_max(
                    stt1[0:1, dst:dst + 1], bank(5, lo=0, hi=1), AX.X),
                    waits=[mms, tk_r], inc=True)
                bank_tok[5] = tk_r
                if not first:
                    tk_r = p.emit("vector", lambda h, which=which: h.tensor_max(
                        stt1[0:1, which:which + 1], stt1[0:1, which:which + 1],
                        stt1[0:1, which + 2:which + 3]), waits=[tk_r], inc=True)
                first = False
            tk_kmax = tk_r
        tk_b2v = p.emit("vector", lambda h: h.tensor_mul(stt1[0:1, 4:5], stt1[0:1, 0:1], stt1[0:1, 1:2]),
                        waits=[tk_qmax, tk_kmax], inc=True)
        tk_bsq = p.emit("scalar", lambda h: h.activation(stt1[0:1, 5:6], stt1[0:1, 4:5], AF.Sqrt),
                        waits=[tk_b2v], inc=True)
        tk_nb = p.emit("vector", lambda h: h.tensor_copy(negb[0:1, 0:1], stt1[0:1, 5:6]),
                       waits=[tk_bsq], inc=True)
        tk_nbmm = p.emit("tensor", lambda h: h.matmul(bank(5, lo=0, hi=128, w=1), onesb[0:1, :],
                                                      negb[0:1, 0:1], start=True, stop=True),
                         waits=[tk_nb, bank_tok[5]], inc=True)
        tk_negB = p.emit("scalar", lambda h: h.activation(negB_b[:, :], bank(5, w=1), AF.Copy,
                                                          bias=0.0, scale=-1.0),
                         waits=[tk_nbmm], inc=True)
        bank_tok[5] = tk_negB

        if "qkv" in dbg:
            dmp["q"] = p.dma("gpsimd", "dbg_q", dbg_dram("q", [128, NCH * T], BF16), q_fm[:, :, :], waits=[tk_q_evac])
            dmp["kv"] = p.dma("gpsimd", "dbg_kv", dbg_dram("kv", [128, T], BF16), kv_fm[:, :], waits=[tk_kv_evac])
            dmp["vaug"] = p.dma("gpsimd", "dbg_vaug", dbg_dram("vaug", [128, SC * 65], BF16), v_aug[:, :], waits=[tk_vaug])
            p.dma("gpsimd", "dbg_negB", dbg_dram("negB", [128, 1]), negB_b[:, :], waits=[tk_negB])

        ost_tok = [None] * 16
        ohi_toks = [None] * 16
        zr_hi_tok = {32: None, 64: None}
        zr_lo_tok = None
        zq_bank_tok = None
        sc2_tok = None
        pv_bank_tok = {4: bank_tok[4], 5: bank_tok[5]}
        zq_bank_tok = bank_tok[6]
        dbg_pt_done = [False]

        pv_prev = [None, None]
        scat_queue = []

        def make_scatter(tau, ch, oev_tok, ohi_tok, zib_tok, zbk=7):
            def emit_scatter():
                nonlocal zr_lo_tok
                tk_zb = p.emit("tensor", lambda h, ch=ch, zbk=zbk: h.matmul(
                    bank(zbk), e16t[:, ch * 128:(ch + 1) * 128], zr_st[0:16, 0:512],
                    start=True, stop=True),
                    waits=[zib_tok, tk_e16, bank_tok[zbk]], inc=True)
                zr_lo_tok = tk_zb
                last_rd = None
                UH = UB // 2
                cpr0 = (tau * UB) >> 1
                dstE = A_fm[0:64, :, :].rearrange("p c (k s) -> p c s k", s=16)[
                    :, cpr0:cpr0 + UH, ch, :]
                dstO = A_fm[64:128, :, :].rearrange("p c (k s) -> p c s k", s=16)[
                    :, cpr0:cpr0 + UH, ch, :]
                oev_e = O_st[0:64, ch * 512:(ch + 1) * 512].rearrange(
                    "p (uu k) -> p uu k", k=HB)[:, 0::2, :]
                ohi_o = O_hi[64:128, ch * 256:(ch + 1) * 256].rearrange(
                    "p (uu k) -> p uu k", k=HB)
                zbE = bank(zbk, lo=0, hi=64)[:, :].rearrange(
                    "p (uu k) -> p uu k", k=HB)[:, 0::2, :]
                zbO = bank(zbk, lo=64, hi=128)[:, :].rearrange(
                    "p (uu k) -> p uu k", k=HB)[:, 1::2, :]
                p.emit("vector", lambda h: h.tensor_tensor(dstE, oev_e, zbE, ALU.mult),
                       waits=[tk_zb, oev_tok, dmp.get("r1")], inc=False)
                last_rd = p.emit("vector", lambda h: h.tensor_tensor(dstO, ohi_o, zbO, ALU.mult),
                                 waits=[tk_zb, ohi_tok, dmp.get("r1")], inc=True)
                ost_tok[ch] = last_rd
                bank_tok[zbk] = last_rd
            return emit_scatter

        zbs_war = [None]
        for tau in range(NTAU):
            oev_toks = {}
            zq_last = None
            for c in range(8):
                if scat_queue:
                    scat_queue.pop(0)()
                expA = [None] * SC
                expB = [None] * SC
                pvA = pvB = None
                for j0 in range(0, SC + 2, 2):
                    if j0 == (SC // 2) and scat_queue:
                        scat_queue.pop(0)()
                    for j in (j0, j0 + 1):
                        if j < SC:
                            bA = j % 2
                            bB = 2 + j % 2
                            tkp = p.emit("tensor", lambda h, j=j, tau=tau, c=c, bA=bA: h.matmul(
                                bank(bA), kv_fm[0:64, j * 128:(j + 1) * 128],
                                q_fm[0:64, c, tau * 512:(tau + 1) * 512],
                                start=True, stop=True, tile_position=(0, 0)),
                                waits=[tk_kv_evac, tk_q_evac, bank_tok[bA]], inc=True)
                            tkA2 = tkp
                            if j % 2 == 1:
                                qkA = tkp
                            tkp = p.emit("tensor", lambda h, j=j, tau=tau, c=c, bB=bB: h.matmul(
                                bank(bB), khi[64:128, j * 128:(j + 1) * 128],
                                q_fm[64:128, c, tau * 512:(tau + 1) * 512],
                                start=True, stop=True, tile_position=(64, 0)),
                                waits=[tk_kvx, bank_tok[bB]], inc=True)
                            if j % 2 == 1:
                                qkB = tkp
                            eA = p.emit("scalar", lambda h, j=j, bA=bA: h.activation(
                                PT_A[:, j * 512:(j + 1) * 512], bank(bA), AF.Exp,
                                bias=negB_b[:, 0:1], scale=1.0),
                                waits=[tkA2, tk_negB, pv_prev[0], dmp.get("pt")], inc=True)
                            expA[j] = eA
                            eB = p.emit("scalar", lambda h, j=j, bB=bB: h.activation(
                                PT_B[:, j * 512:(j + 1) * 512], bank(bB), AF.Exp,
                                bias=negB_b[:, 0:1], scale=1.0),
                                waits=[tkp, pv_prev[1], dmp.get("pt")], inc=True)
                            expB[j] = eB
                            bank_tok[bA] = eA
                            bank_tok[bB] = eB
                    for j in (j0 - 2, j0 - 1):
                        if 0 <= j < SC:
                            jj = j
                            pvA = p.emit("tensor", lambda h, jj=jj: h.matmul(
                                bank(4, lo=0, hi=65), v_aug[:, jj * 65:(jj + 1) * 65],
                                PT_A[:, jj * 512:(jj + 1) * 512],
                                start=(jj == 0), stop=(jj == SC - 1)),
                                waits=[tk_vaug, expA[jj], pv_bank_tok[4] if jj == 0 else None],
                                inc=(jj == SC - 1))
                    for j in (j0 - 2, j0 - 1):
                        if 0 <= j < SC:
                            jj = j
                            pvB = p.emit("tensor", lambda h, jj=jj: h.matmul(
                                bank(5, lo=0, hi=65), v_aug[:, jj * 65:(jj + 1) * 65],
                                PT_B[:, jj * 512:(jj + 1) * 512],
                                start=(jj == 0), stop=(jj == SC - 1)),
                                waits=[expB[jj], pv_bank_tok[5] if jj == 0 else None],
                                inc=(jj == SC - 1))
                if "pt" in dbg and tau == 0 and c == 0 and not dbg_pt_done[0]:
                    dbg_pt_done[0] = True
                    dmp["pt"] = p.dma("gpsimd", "dbg_ptA", dbg_dram("ptA", [128, SC * 512], BF16),
                                      PT_A[:, :], waits=[pvA])
                pv_prev = [pvA, pvB]
                for ch, tk_pv, bpv in ((2 * c, pvA, 4), (2 * c + 1, pvB, 5)):
                    zrow = 64 if ch % 2 == 0 else 32
                    tk_oev = p.emit("vector", lambda h, ch=ch, bpv=bpv: h.tensor_copy(
                        O_st[0:64, ch * 512:ch * 512 + 512], bank(bpv, lo=0, hi=64)),
                        waits=[tk_pv, ost_tok[ch]], inc=True)
                    oev_toks[ch] = tk_oev
                    if UB >= 2:
                        ohi_toks[ch] = p.dma("sync", f"oh{ch}",
                            O_hi[64:128, ch * 256:(ch + 1) * 256].rearrange(
                                "p (uu k) -> p uu k", k=HB),
                            O_st[0:64, ch * 512:(ch + 1) * 512].rearrange(
                                "p (uu k) -> p uu k", k=HB)[:, 1::2, :],
                            waits=[tk_oev, ost_tok[ch]])
                    tk_zev = p.emit("vector", lambda h, bpv=bpv, zrow=zrow: h.tensor_copy(
                        zr_st[zrow:zrow + 1, :], bank(bpv, lo=64, hi=65)),
                        waits=[tk_pv, zr_hi_tok[zrow]], inc=True)
                    tk_zq = p.emit("tensor", lambda h, ch=ch, zrow=zrow: h.matmul(
                        bank(6, lo=0, hi=16), zsel16[zrow:zrow + 1, ch * 16:(ch + 1) * 16],
                        zr_st[zrow:zrow + 1, :],
                        start=(ch == 0), stop=(ch == 15)),
                        waits=[tk_zev, tk_zsel] + ([zq_bank_tok] if ch == 0 else []),
                        inc=True)
                    zr_hi_tok[zrow] = tk_zq
                    zq_last = tk_zq
                    pv_bank_tok[bpv] = tk_zev
            tk_zi = p.emit("vector", lambda h: h.reciprocal(zinv32[:, :], bank(6, lo=0, hi=16)),
                           waits=[zq_last, dmp.get("zq")], inc=True)
            zq_bank_tok = tk_zi
            tk_zib = p.emit("vector", lambda h: h.tensor_copy(zr_st[0:16, 0:512], zinv32[:, :]),
                            waits=[tk_zi, zr_lo_tok], inc=True)
            if "zq" in dbg and tau == 0:
                dmp["zq"] = p.dma("gpsimd", "dbg_zinv", dbg_dram("zinv", [16, 512]), zinv32[:, :], waits=[tk_zi])
            for ch in range(16):
                zbk = 7 if tau < NTAU - 1 else ch % 8
                scat_queue.append(make_scatter(tau, ch, oev_toks[ch], ohi_toks[ch], tk_zib, zbk))
        afm_partial = p.etok("vector")
        tk_pt_dead = p.etok("tensor")
        last_scat_tok = {}
        ich = 0
        while scat_queue:
            scat_queue.pop(0)()
            last_scat_tok[ich] = ost_tok[ich]
            ich += 1
        tk_afm_done = p.etok("vector")
        wb20_free = last_scat_tok.get(7, tk_afm_done)
        tk_b2T = p.dma("gpsimd", "sml2", b2T[:, :], b2_d,
                       waits=[tk_afm_done, p.etok("tensor")])
        tk_e80 = p.emit("gpsimd", lambda h: h.memset(e8[:, :], 1.0),
                        waits=[tk_afm_done, p.etok("tensor")], inc=True)
        tk_e8 = p.emit("gpsimd", lambda h: h.affine_select(
            out=e8[:, :], in_=e8[:, :], pattern=[[1, 8], [0, 128]],
            compare_op=ALU.is_equal, fill=0.0, base=0, channel_multiplier=-1),
            waits=[tk_e80], inc=True)
        if "afm" in dbg:
            dmp["afm"] = p.dma("gpsimd", "dbg_afm", dbg_dram("afm", [128, NCH * T], BF16), A_fm[:, :, :],
                               waits=[tk_afm_done])

        wo_slot_last = [None, None]
        res_tok = [None] * NTAU
        wo_epi = []

        def make_wo_epi2(c_o, tau, ev_tok):
            def emit_epi():
                bt = 4 + (c_o + tau) % 2
                tts = None
                for ss in range(4):
                    tts = p.emit("tensor", lambda h, c_o=c_o, tau=tau, ss=ss, bt=bt: h.transpose(
                        bankb(bt)[:, ss * 128:(ss + 1) * 128],
                        s_big[:, (c_o % 2) * NTAU * 512 + tau * 512 + ss * 128:
                              (c_o % 2) * NTAU * 512 + tau * 512 + (ss + 1) * 128],
                        ident[:, :]),
                        waits=[ev_tok, bank_tok[bt]], inc=(ss == 3))
                xv = x_tm[:, tau * 4096:(tau + 1) * 4096].rearrange(
                    "p (s r) -> p s r", r=1024)[:, :, c_o * 128:(c_o + 1) * 128]
                tk_r = p.emit("vector", lambda h, xv=xv, bt=bt: h.tensor_add(
                    xv, xv, bankb(bt)[:, 0:512].rearrange("p (s k) -> p s k", k=128)),
                    waits=[tts], inc=True)
                bank_tok[bt] = tk_r
                res_tok[tau] = tk_r
                sbig_war[c_o % 2] = p.etok("tensor")
            return emit_epi

        sbig_war = [None, None]
        tk_wo23 = p.dma("gpsimd", "wo2", woq23[:, 0:2048], wo_d[2],
                        waits=[p.etok("tensor"), p.etok("vector")])
        tk_wo23b = p.dma("gpsimd", "wo3", woq23[:, 2048:4096], wo_d[3],
                        waits=[p.etok("tensor"), p.etok("vector")])
        for tau0 in range(0, NTAU, 2):
            TP = min(2, NTAU - tau0)
            for c_o in range(NCH):
                qq = c_o // 2
                if qq < 2:
                    lsrc = wos[qq % 2]
                    lbase = 0
                    wtok = tk_wo[qq]
                else:
                    lsrc = woq23
                    lbase = (qq - 2) * 2048
                    wtok = tk_wo23 if qq == 2 else tk_wo23b
                mmt2 = [None] * TP
                for c_a in range(NCH):
                    for dt in range(TP):
                        tau = tau0 + dt
                        b = (c_o % 2) * 2 + dt
                        mmt2[dt] = p.emit("tensor", lambda h, c_o=c_o, tau=tau, c_a=c_a, b=b,
                                          lsrc=lsrc, lbase=lbase: h.matmul(
                            bank(b), lsrc[:, lbase + c_a * 256 + (c_o % 2) * 128:
                                          lbase + c_a * 256 + (c_o % 2) * 128 + 128],
                            A_fm[:, c_a, tau * 512:(tau + 1) * 512],
                            start=(c_a == 0), stop=(c_a == NCH - 1)),
                            waits=[afm_partial if c_a < 2 * (NTAU - 1) else tk_afm_done,
                                   wtok, bank_tok[b] if c_a == 0 else None],
                            inc=(c_a == NCH - 1))
                for dt in range(TP):
                    tau = tau0 + dt
                    b = (c_o % 2) * 2 + dt
                    tk_ev = p.emit("scalar", lambda h, c_o=c_o, tau=tau, b=b: h.activation(
                        s_big[:, (c_o % 2) * NTAU * 512 + tau * 512:
                              (c_o % 2) * NTAU * 512 + (tau + 1) * 512],
                        bank(b), AF.Identity, bias=bo_s[:, c_o:c_o + 1], scale=1.0),
                        waits=[mmt2[dt], tk_cst, sbig_war[c_o % 2]], inc=True)
                    bank_tok[b] = tk_ev
                    wo_epi.append(make_wo_epi2(c_o, tau, tk_ev))
                while len(wo_epi) > 1:
                    wo_epi.pop(0)()
        while wo_epi:
            wo_epi.pop(0)()
        tk_x1 = p.etok("vector")
        if "attn" in dbg:
            dmp["x1"] = p.dma("gpsimd", "dbg_x1", dbg_dram("x1", [128, NT * 1024]), x_tm[:, :], waits=[tk_x1])

        if dmp.get("q") is not None:
            p.emit("vector", lambda h: h.wait_ge(dmp["q"][0], dmp["q"][1]))
        xn2_touts = ln(xn2_fm, tk_x1, sq_war=p.etok("scalar"),
                       x_toks=[res_tok[i // max(NT // NTAU, 1)] for i in range(NT)])
        TPG2 = max(NT // NTAU, 1)
        xn2_tok = [xn2_touts[(g + 1) * TPG2 - 1] for g in range(NTAU)]
        tk_xn2 = xn2_touts[-1]

        tk_gt = None
        gslot_war = [None, None]
        gate_mm_toks = {}
        gate_cl = []

        def g_mm(tau):
            def f():
                b = tau % 2
                mms = None
                for c_e in range(NCH):
                    mms = p.emit("tensor", lambda h, tau=tau, c_e=c_e, b=b: h.matmul(
                        bank(b, lo=0, hi=8), gws[:, c_e * 8:(c_e + 1) * 8],
                        xn2_fm[:, c_e, tau * 512:(tau + 1) * 512],
                        start=(c_e == 0), stop=(c_e == NCH - 1)),
                        waits=[xn2_tok[tau], tk_sml, bank_tok[b]], inc=(c_e == NCH - 1))
                gate_mm_toks[tau] = mms
                bank_tok[b] = mms
            return f

        def g_tr(tau):
            def f():
                nonlocal tk_gt
                b = tau % 2
                tk_ge = None
                for s4 in range(4):
                    i = tau * 4 + s4
                    sl = i % 2
                    b2 = 2 + i % 2
                    tk_ge = p.emit("scalar", lambda h, s4=s4, sl=sl, b=b: h.activation(
                        G_sb[0:8, sl * 128:(sl + 1) * 128],
                        bank(b, lo=0, hi=8)[:, s4 * 128:(s4 + 1) * 128], AF.Identity,
                        bias=gb_s[0:8, 0:1], scale=1.0),
                        waits=[gate_mm_toks[tau], tk_cst, gslot_war[sl]], inc=True)
                    tts = p.emit("tensor", lambda h, sl=sl, b2=b2: h.transpose(
                        bank(b2, lo=0, hi=128, w=8), G_sb[0:8, sl * 128:(sl + 1) * 128],
                        identf[0:8, 0:8]),
                        waits=[tk_ge, tk_idf, bank_tok[b2]], inc=True)
                    gslot_war[sl] = tts
                    tk_gt = p.emit("vector", lambda h, i=i, b2=b2: h.tensor_copy(
                        Gtm[:, i * 8:(i + 1) * 8], bank(b2, w=8)),
                        waits=[tts, tk_afm_done], inc=True)
                    bank_tok[b2] = tk_gt
                bank_tok[b] = tk_ge
            return f

        def g_soft():
            nonlocal tk_gt
            GV = Gtm[:, :].rearrange("p (b j) -> p b j", j=8)
            PV_ = probs[:, :].rearrange("p (b j) -> p b j", j=8)
            WV = gtmp[:, :].rearrange("p (b j) -> p b j", j=8)
            def bc(t):
                return t[:, :].unsqueeze(2).broadcast_to([128, NT, 8])
            t1 = p.emit("vector", lambda h: h.reduce_max(gm1[:, :], GV, AX.X),
                        waits=[tk_gt], inc=True)
            t2 = p.emit("vector", lambda h: h.tensor_tensor(PV_, GV, bc(gm1), ALU.subtract),
                        waits=[t1], inc=True)
            t3 = p.emit("scalar", lambda h: h.activation(probs[:, :], probs[:, :], AF.Exp),
                        waits=[t2], inc=True)
            t4 = p.emit("vector", lambda h: h.reduce_sum(gm2[:, :], PV_, AX.X),
                        waits=[t3], inc=True)
            t5 = p.emit("vector", lambda h: h.reciprocal(gm1[:, :], gm2[:, :]),
                        waits=[t4], inc=True)
            t6 = p.emit("vector", lambda h: h.tensor_tensor(PV_, PV_, bc(gm1), ALU.mult),
                        waits=[t5], inc=True)
            t7 = p.emit("vector", lambda h: h.reduce_max(gm1[:, :], PV_, AX.X),
                        waits=[t6], inc=True)
            t8 = p.emit("vector", lambda h: h.tensor_tensor(WV, PV_, bc(gm1), ALU.is_ge),
                        waits=[t7], inc=True)
            t9 = p.emit("vector", lambda h: h.scalar_tensor_tensor(
                gtmp[:, :], gtmp[:, :], -1e30, probs[:, :], ALU.mult, ALU.add),
                waits=[t8], inc=True)
            t10 = p.emit("vector", lambda h: h.reduce_max(gm2[:, :], WV, AX.X),
                         waits=[t9], inc=True)
            t11 = p.emit("vector", lambda h: h.tensor_tensor(WV, PV_, bc(gm2), ALU.is_ge),
                         waits=[t10], inc=True)
            t12 = p.emit("vector", lambda h: h.tensor_tensor(
                Wtm[:, :], probs[:, :], gtmp[:, :], ALU.mult),
                waits=[t11], inc=True)
            return t12

        def g_wtmT(i0, tk_wtm):
            def f():
                nonlocal tk_wem
                for i in range(i0, min(i0 + 4, NT)):
                    b = i % 2
                    tts = p.emit("tensor", lambda h, i=i, b=b: h.transpose(
                        bankb(b)[0:8, 0:128], Wtm[:, i * 8:(i + 1) * 8], ident[:, :]),
                        waits=[tk_wtm, bank_tok[b]], inc=True)
                    tk_wem = p.emit("vector", lambda h, i=i, b=b: h.tensor_copy(
                        W_em[0:8, i * 128:(i + 1) * 128], bankb(b)[0:8, 0:128]),
                        waits=[tts, tk_afm_done], inc=True)
                    bank_tok[b] = tk_wem
            return f

        tk_wem = None

        def g_full(tau):
            def f():
                g_mm(tau)()
                g_tr(tau)()
            return f

        for tau in range(NTAU):
            gate_cl.append(g_full(tau))

        tk_seed = None
        def emit_web(e):
            tk_webl = None
            for tau in range(NTAU):
                b = tau % 2
                tk_wm = p.emit("tensor", lambda h, e=e, tau=tau, b=b: h.matmul(
                    bank(b), e8[0:8, e * 128:(e + 1) * 128],
                    W_em[0:8, tau * 512:(tau + 1) * 512],
                    start=True, stop=True),
                    waits=[tk_wem, tk_e8, bank_tok[b]], inc=True)
                tk_webl = p.emit("vector", lambda h, tau=tau, b=b: h.tensor_copy(
                    w_eb[:, tau * 512:(tau + 1) * 512], bank(b)),
                    waits=[tk_wm, tk_web_war, tk_afm_done], inc=True)
                bank_tok[b] = tk_webl
            return tk_webl

        tk_w1 = {}
        tk_w2 = {}
        w1_slot_last = [None, None]
        w2_slot_last = [None, None]

        def w1_stream(he, q, *wait_toks):
            buf = wb1b[q % 2] if he == 0 else wb1[q % 2]
            st = ("w1c" if q % 2 == 0 else "w1d") if he == 0 else ("w1a" if q % 2 == 0 else "w1b")
            tk_w1[(he, q)] = p.dma("gpsimd", st, buf[:, :], w1_d[he, q], waits=list(wait_toks))

        def w2_stream(he, q, *wait_toks):
            tk_w2[(he, q)] = p.dma("gpsimd", "w2a" if q % 2 == 0 else "w2b",
                                   wb2[q % 2][:, :], w2_d[he, q], waits=list(wait_toks))

        tk_free_r4 = p.etok("tensor")
        w1_stream(0, 0, tk_pt_dead, dmp.get("kv"), dmp.get("vaug"))
        w1_stream(0, 1, tk_pt_dead, dmp.get("kv"), dmp.get("vaug"))
        w2_stream(0, 0, wb20_free, p.etok("tensor"), dmp.get("kv"), dmp.get("vaug"))
        w2_stream(0, 1, tk_free_r4, tk_afm_done, dmp.get("kv"), dmp.get("vaug"))

        tk_hs_last = None
        tk_oacc = None
        tk_web_war = None
        oacc_co_tok = {}
        seed_toks = {}

        def emit_final_res(c_o, grp_free):
            for tau in range(NTAU):
                bt = grp_free + tau % 4
                tts = None
                for ss in range(4):
                    tts = p.emit("tensor", lambda h, c_o=c_o, tau=tau, ss=ss, bt=bt: h.transpose(
                        bankb(bt)[:, ss * 128:(ss + 1) * 128],
                        o_acc[:, c_o, tau * 512 + ss * 128: tau * 512 + (ss + 1) * 128],
                        ident[:, :]),
                        waits=[oacc_co_tok[c_o], bank_tok[bt]], inc=(ss == 3))
                xv = x_tm[:, tau * 4096:(tau + 1) * 4096].rearrange(
                    "p (s r) -> p s r", r=1024)[:, :, c_o * 128:(c_o + 1) * 128]
                tk_r2 = p.emit("vector", lambda h, xv=xv, bt=bt: h.tensor_add(
                    xv, xv, bankb(bt)[:, 0:512].rearrange("p (s k) -> p s k", k=128)),
                    waits=[tts], inc=True)
                bank_tok[bt] = tk_r2

        for he in range(16):
            e = he // 2
            half = he % 2
            hs_tok = {}
            for c_h in range(8):
                if he == 0:
                    if gate_cl:
                        gate_cl.pop(0)()
                    if not gate_cl and c_h == 4:
                        tk_wtm_f = g_soft()
                        for i0 in range(0, NT, 4):
                            gate_cl.append(g_wtmT(i0, tk_wtm_f))
                if c_h == 1 and half == 0 and he > 0:
                    tk_web = emit_web(e)
                q = c_h // 2
                grp = ((c_h + 1) % 2) * 4
                mmt = [None] * NTAU
                for c_f in range(NCH):
                    for tau in range(NTAU):
                        b = grp + tau
                        wsrc = wb1b[q % 2] if he == 0 else wb1[q % 2]
                        mmt[tau] = p.emit("tensor", lambda h, c_h=c_h, tau=tau, c_f=c_f, b=b, q=q, he=he, wsrc=wsrc: h.matmul(
                            bank(b), wsrc[:, c_f * 256 + (c_h % 2) * 128: c_f * 256 + (c_h % 2) * 128 + 128],
                            xn2_fm[:, c_f, tau * 512:(tau + 1) * 512],
                            start=(c_f == 0), stop=(c_f == NCH - 1)),
                            waits=[xn2_tok[tau], tk_w1[(he, q)], bank_tok[b] if c_f == 0 else None],
                            inc=(c_f == NCH - 1))
                bidx = e * 16 + half * 8 + c_h
                for tau in range(NTAU):
                    b = grp + tau
                    tk_hev = p.emit("scalar", lambda h, c_h=c_h, tau=tau, b=b, bidx=bidx: h.activation(
                        Hs[:, c_h, tau * 512:(tau + 1) * 512], bank(b), AF.Relu,
                        bias=b1_s[:, bidx:bidx + 1], scale=1.0),
                        waits=[mmt[tau], tk_cst, dmp.get("afm")], inc=True)
                    bank_tok[b] = tk_hev
                    tk_hs_last = tk_hev
                    hs_tok[c_h] = tk_hev
                    w1_slot_last[q % 2] = tk_hev
                if c_h % 2 == 1:
                    nq = q + 2
                    nhe, nqq = (he, nq) if nq < 4 else (he + 1, nq - 4)
                    if nhe < 16:
                        w1_stream(nhe, nqq, w1_slot_last[nq % 2])
            if he == 0:
                while gate_cl:
                    gate_cl.pop(0)()
                tk_web = emit_web(0)
            for c_o in range(NCH):
                q = c_o // 2
                grp = 4 if c_o % 2 == 0 else 0
                if he == 0:
                    sgrp = grp
                    seed_toks[c_o] = []
                    for tau in range(NTAU):
                        b = sgrp + tau
                        tk_sm = p.emit("tensor", lambda h, c_o=c_o, tau=tau, b=b: h.matmul(
                            bank(b), b2T[0:8, c_o * 128:(c_o + 1) * 128],
                            W_em[0:8, tau * 512:(tau + 1) * 512],
                            start=True, stop=True),
                            waits=[tk_wem, tk_b2T, bank_tok[b]], inc=True)
                        tk_seed = p.emit("vector", lambda h, c_o=c_o, tau=tau, b=b: h.tensor_copy(
                            o_acc[:, c_o, tau * 512:(tau + 1) * 512], bank(b)),
                            waits=[tk_sm, p.etok("scalar")], inc=True)
                        bank_tok[b] = tk_seed
                        seed_toks[c_o].append(tk_seed)
                mms = None
                for c_h in range(8):
                    for tau in range(NTAU):
                        b = grp + tau
                        mms = p.emit("tensor", lambda h, c_o=c_o, c_h=c_h, tau=tau, b=b, q=q, he=he: h.matmul(
                            bank(b), wb2[q % 2][:, c_h * 256 + (c_o % 2) * 128: c_h * 256 + (c_o % 2) * 128 + 128],
                            Hs[:, c_h, tau * 512:(tau + 1) * 512],
                            start=(c_h == 0), stop=(c_h == 7)),
                            waits=[hs_tok[c_h], tk_w2[(he, q)]] +
                                  ([bank_tok[b]] if c_h == 0 else []),
                            inc=(c_h == 7 and tau == NTAU - 1))
                tk_tmp = p.emit("vector", lambda h, grp=grp: h.tensor_tensor(
                    tmpb[:, :], pw[:, grp * 512:(grp + NTAU) * 512], w_eb[:, 0:NTAU * 512], ALU.mult),
                    waits=[mms, tk_web, tk_oacc], inc=True)
                tk_oacc = p.emit("vector", lambda h, c_o=c_o: h.tensor_add(
                    o_acc[:, c_o, :], o_acc[:, c_o, :], tmpb[:, 0:T]),
                    waits=[tk_tmp, seed_toks[c_o][-1]], inc=True)
                tk_web_war = tk_tmp
                for tau in range(NTAU):
                    bank_tok[grp + tau] = tk_oacc
                w2_slot_last[q % 2] = tk_oacc
                if he == 15:
                    oacc_co_tok[c_o] = tk_oacc
                    if c_o >= 1:
                        emit_final_res(c_o - 1, 4 - grp if grp in (0, 4) else 0)
                if c_o % 2 == 1:
                    nq = q + 2
                    nhe, nqq = (he, nq) if nq < 4 else (he + 1, nq - 4)
                    if nhe < 16:
                        w2_stream(nhe, nqq, w2_slot_last[nq % 2])
        if "ffnacc" in dbg:
            p.dma("gpsimd", "dbg_oacc", dbg_dram("oacc", [128, NCH * T], BF16), o_acc[:, :, :],
                  waits=[tk_oacc])

        emit_final_res(NCH - 1, 0)
        p.dma("sync", "out", y_d, x_tm[:, :], waits=[p.etok("vector")])
        for snm in list(p.dma_sem.keys()):
            if p.dma_cnt[snm]:
                p.emit("sync", lambda h, snm=snm: h.wait_ge(p.dma_sem[snm], p.dma_cnt[snm]))

        p.finalize()
    return nc, dbg_out


def prep_inputs(ins, T=2048):
    NT = T // 128
    o = {}
    x = np.ascontiguousarray(np.asarray(ins["x"]), dtype=np.float32)
    o["x"] = np.ascontiguousarray(x.reshape(NT, 128, 1024).transpose(1, 0, 2).reshape(128, NT * 1024))
    wq = np.ascontiguousarray(np.asarray(ins["Wq"]).T)
    o["wq"] = np.ascontiguousarray(
        wq.reshape(8, 128, 4, 256).transpose(2, 1, 0, 3).reshape(4, 128, 8 * 256))
    wo = np.ascontiguousarray(np.asarray(ins["Wo"]).T)
    o["wo"] = np.ascontiguousarray(
        wo.reshape(8, 128, 4, 256).transpose(2, 1, 0, 3).reshape(4, 128, 8 * 256))
    wkv = np.ascontiguousarray(np.asarray(ins["Wkv"]).T)
    o["wkv"] = np.ascontiguousarray(wkv.reshape(8, 128, 128).transpose(1, 0, 2).reshape(128, 8 * 128))
    o["bq"] = np.ascontiguousarray(np.asarray(ins["bq"]).reshape(8, 128).T)
    o["bo"] = np.ascontiguousarray(np.asarray(ins["bo"]).reshape(8, 128).T)
    o["bkv"] = np.ascontiguousarray(np.asarray(ins["bkv"]).reshape(128, 1))
    gw = np.ascontiguousarray(np.asarray(ins["gate_w"]).T)
    o["gw"] = np.ascontiguousarray(gw.reshape(8, 128, 8).transpose(1, 0, 2).reshape(128, 64))
    o["gb"] = np.ascontiguousarray(np.asarray(ins["gate_b"]).reshape(8, 1))
    b1 = np.asarray(ins["b1"])
    o["b1"] = np.ascontiguousarray(b1.reshape(8, 16, 128).transpose(2, 0, 1).reshape(128, 128))
    o["b2"] = np.ascontiguousarray(np.asarray(ins["b2"]))
    w1 = np.asarray(ins["w1"])
    w1p = np.empty((16, 4, 128, 8 * 256), np.float32)
    for e in range(8):
        for half in range(2):
            for q in range(4):
                blk = w1[e][:, half * 1024 + q * 256: half * 1024 + (q + 1) * 256]
                w1p[e * 2 + half, q] = blk.reshape(8, 128, 256).transpose(1, 0, 2).reshape(128, 8 * 256)
    o["w1"] = w1p
    w2 = np.asarray(ins["w2"])
    w2p = np.empty((16, 4, 128, 8 * 256), np.float32)
    for e in range(8):
        for half in range(2):
            for q in range(4):
                blk = w2[e][half * 1024:(half + 1) * 1024, q * 256:(q + 1) * 256]
                w2p[e * 2 + half, q] = blk.reshape(8, 128, 256).transpose(1, 0, 2).reshape(128, 8 * 256)
    o["w2"] = w2p
    return o


def unprep_output(y, T=2048):
    NT = T // 128
    return np.ascontiguousarray(np.asarray(y).reshape(128, NT, 1024).transpose(1, 0, 2).reshape(T, 1024))


def prep_x(x, T=2048):
    NT = T // 128
    x = np.ascontiguousarray(np.asarray(x), dtype=np.float32)
    return np.ascontiguousarray(x.reshape(NT, 128, 1024).transpose(1, 0, 2).reshape(128, NT * 1024))


def prep_inputs(ins, T=2048):
    NT = T // 128
    o = {}
    x = np.ascontiguousarray(np.asarray(ins["x"]), dtype=np.float32)
    o["x"] = np.ascontiguousarray(x.reshape(NT, 128, 1024).transpose(1, 0, 2).reshape(128, NT * 1024))
    wq = np.ascontiguousarray(np.asarray(ins["Wq"]).T)
    o["wq"] = np.ascontiguousarray(
        wq.reshape(8, 128, 4, 256).transpose(2, 1, 0, 3).reshape(4, 128, 8 * 256))
    wo = np.ascontiguousarray(np.asarray(ins["Wo"]).T)
    o["wo"] = np.ascontiguousarray(
        wo.reshape(8, 128, 4, 256).transpose(2, 1, 0, 3).reshape(4, 128, 8 * 256))
    wkv = np.ascontiguousarray(np.asarray(ins["Wkv"]).T)
    o["wkv"] = np.ascontiguousarray(wkv.reshape(8, 128, 128).transpose(1, 0, 2).reshape(128, 8 * 128))
    o["bq"] = np.ascontiguousarray(np.asarray(ins["bq"]).reshape(8, 128).T)
    o["bo"] = np.ascontiguousarray(np.asarray(ins["bo"]).reshape(8, 128).T)
    o["bkv"] = np.ascontiguousarray(np.asarray(ins["bkv"]).reshape(128, 1))
    gw = np.ascontiguousarray(np.asarray(ins["gate_w"]).T)
    o["gw"] = np.ascontiguousarray(gw.reshape(8, 128, 8).transpose(1, 0, 2).reshape(128, 64))
    o["gb"] = np.ascontiguousarray(np.asarray(ins["gate_b"]).reshape(8, 1))
    b1 = np.asarray(ins["b1"])
    o["b1"] = np.ascontiguousarray(b1.reshape(8, 16, 128).transpose(2, 0, 1).reshape(128, 128))
    o["b2"] = np.ascontiguousarray(np.asarray(ins["b2"]))
    w1 = np.asarray(ins["w1"])
    w1p = np.empty((16, 4, 128, 8 * 256), np.float32)
    for e in range(8):
        for half in range(2):
            for q in range(4):
                blk = w1[e][:, half * 1024 + q * 256: half * 1024 + (q + 1) * 256]
                w1p[e * 2 + half, q] = blk.reshape(8, 128, 256).transpose(1, 0, 2).reshape(128, 8 * 256)
    o["w1"] = w1p
    w2 = np.asarray(ins["w2"])
    w2p = np.empty((16, 4, 128, 8 * 256), np.float32)
    for e in range(8):
        for half in range(2):
            for q in range(4):
                blk = w2[e][half * 1024:(half + 1) * 1024, q * 256:(q + 1) * 256]
                w2p[e * 2 + half, q] = blk.reshape(8, 128, 256).transpose(1, 0, 2).reshape(128, 8 * 256)
    o["w2"] = w2p
    return o


def unprep_output(y, T=2048):
    NT = T // 128
    return np.ascontiguousarray(np.asarray(y).reshape(128, NT, 1024).transpose(1, 0, 2).reshape(T, 1024))

from concourse.bass_utils import run_bass_kernel_spmd

_T = 2048


def kernel(L0_attn_x, L0_attn_Wq, L0_attn_bq, L0_attn_Wkv, L0_attn_bkv,
           L0_attn_Wo, L0_attn_bo, L0_ffn_gate_w, L0_ffn_gate_b,
           L0_ffn_w1, L0_ffn_b1, L0_ffn_w2, L0_ffn_b2):
    x = np.ascontiguousarray(np.asarray(L0_attn_x, dtype=np.float32))
    B = x.shape[0]
    shared = {
        "Wq": L0_attn_Wq, "bq": L0_attn_bq, "Wkv": L0_attn_Wkv, "bkv": L0_attn_bkv,
        "Wo": L0_attn_Wo, "bo": L0_attn_bo, "gate_w": L0_ffn_gate_w,
        "gate_b": L0_ffn_gate_b, "w1": L0_ffn_w1, "b1": L0_ffn_b1,
        "w2": L0_ffn_w2, "b2": L0_ffn_b2,
    }
    base = prep_inputs(dict(shared, x=x[0]), T=_T)
    in_maps = []
    for b in range(B):
        m = dict(base)
        m["x"] = prep_x(x[b], T=_T)
        in_maps.append(m)
    nc, _ = build(T=_T)
    res = run_bass_kernel_spmd(nc, in_maps, core_ids=list(range(B)))
    out = np.empty((B, _T, 1024), np.float32)
    for b in range(B):
        out[b] = unprep_output(np.asarray(res.results[b]["y"]), T=_T)
    return out
